# version 13
# baseline (speedup 1.0000x reference)
import numpy as np
import ml_dtypes
from contextlib import ExitStack

import concourse.bass as bass
import concourse.mybir as mybir
from concourse.alu_op_type import AluOpType as OP
from concourse.bass_utils import run_bass_kernel_spmd

F32 = mybir.dt.float32
BF16 = mybir.dt.bfloat16
AF = mybir.ActivationFunctionType
AX = mybir.AxisListType

D = 2048
S = 2048
KT = 16
NE = 17
FF = 1024
FT = 8
CHUNK = 1024
NCH = S // CHUNK
EPS = 1e-5
NCORES = 8

ENGINES = ("tensor", "vector", "scalar", "gpsimd", "sync")


class Res:
    __slots__ = ("name", "writer", "readers")

    def __init__(self, name, fence=()):
        self.name = name
        self.writer = None
        self.readers = list(fence)


class Token:
    __slots__ = ("sem", "val", "vc")

    def __init__(self, sem, val, vc):
        self.sem = sem
        self.val = val
        self.vc = vc


class KBuild:
    N_DMA_SEMS = {"sync": 6, "scalar": 4, "gpsimd": 4}

    def __init__(self, nc, stack):
        self.nc = nc
        self.sems = {}
        for e in ENGINES:
            self.sems["c_" + e] = stack.enter_context(nc.semaphore("c_" + e))
        self.qsems = {}
        self.rot = {}
        for q, n in self.N_DMA_SEMS.items():
            self.qsems[q] = [f"d_{q}{i}" for i in range(n)]
            for s in self.qsems[q]:
                self.sems[s] = stack.enter_context(nc.semaphore(s))
            self.rot[q] = 0
        self.counts = {s: 0 for s in self.sems}
        self.know = {e: {s: 0 for s in self.sems} for e in ENGINES}
        self.thunks = {e: [] for e in ENGINES}
        self.n_ops = 0
        self.n_waits = 0

    def barrier(self):
        targets = [(s, self.counts[s]) for s in self.sems
                   if self.counts[s] > 0 and not s.startswith("d_gpsimd")]
        for e in ENGINES:
            wl = [(self.sems[s], v) for s, v in targets]

            def th2(engine, wl=wl):
                for sh, v in wl:
                    engine.wait_ge(sh, v)
            self.thunks[e].append(th2)
            know = self.know[e]
            for s, v in targets:
                if know[s] < v:
                    know[s] = v

    def barrier_token_tensor(self):
        sem = "c_tensor"
        return Token(sem, self.counts[sem], {sem: self.counts[sem]})

    def fence(self):
        toks = []
        for e in ENGINES:
            sem = "c_" + e
            if self.counts[sem]:
                toks.append(Token(sem, self.counts[sem], {sem: self.counts[sem]}))
        for s in list(self.sems):
            if (s.startswith("d_") and self.counts[s]
                    and not s.startswith("d_gpsimd")):
                toks.append(Token(s, self.counts[s], {s: self.counts[s]}))
        return toks

    def op(self, eng, fn, reads=(), writes=(), dma=False, extra_deps=()):
        toks = []
        for r in reads:
            if r.writer is not None:
                toks.append(r.writer)
        for w in writes:
            if w.writer is not None:
                toks.append(w.writer)
            toks.extend(w.readers)
        toks.extend(extra_deps)
        know = self.know[eng]
        if dma:
            qs = self.qsems[eng]
            my_sem = qs[self.rot[eng]]
            self.rot[eng] = (self.rot[eng] + 1) % len(qs)
            skip = set()
            auto = (Token(my_sem, self.counts[my_sem], {})
                    if self.counts[my_sem] > 0 else None)
        else:
            my_sem = "c_" + eng
            skip = set()
            auto = None
        best = {}
        for t in toks:
            if t is None or t.sem in skip:
                continue
            if know.get(t.sem, 0) >= t.val:
                continue
            if t.sem not in best or best[t.sem][0] < t.val:
                best[t.sem] = (t.val, t.vc)
        if auto is not None and know.get(auto.sem, 0) < auto.val:
            if auto.sem not in best or best[auto.sem][0] < auto.val:
                best[auto.sem] = (auto.val, auto.vc)
        for sem, (val, vc) in best.items():
            for s2, v2 in vc.items():
                if know[s2] < v2:
                    know[s2] = v2
            if know[sem] < val:
                know[sem] = val
        self.counts[my_sem] += 16 if dma else 1
        newval = self.counts[my_sem]
        vc = dict(know)
        vc[my_sem] = newval
        tok = Token(my_sem, newval, vc)

        wait_list = [(self.sems[s], v) for s, (v, _) in best.items()]
        inc_handle = self.sems[my_sem]
        inc_amt = 16 if dma else 1

        def thunk(engine, wl=wait_list, fn=fn, ih=inc_handle, ia=inc_amt):
            for sh, v in wl:
                engine.wait_ge(sh, v)
            last = fn(engine)
            assert last is not None, "op fn must return last instruction"
            last.then_inc(ih, ia)

        self.thunks[eng].append(thunk)
        self.n_ops += 1
        self.n_waits += len(wait_list)
        for r in reads:
            r.readers.append(tok)
        for w in writes:
            w.writer = tok
            w.readers = []
        return tok

    def finalize(self, block, final_eng="sync", final_tokens=()):
        if final_tokens:
            know = self.know[final_eng]
            best = {}
            for t in final_tokens:
                if t is None or know.get(t.sem, 0) >= t.val:
                    continue
                if t.sem not in best or best[t.sem] < t.val:
                    best[t.sem] = t.val
            wl = [(self.sems[s], v) for s, v in best.items()]

            def fthunk(engine, wl=wl):
                for sh, v in wl:
                    engine.wait_ge(sh, v)
            self.thunks[final_eng].append(fthunk)

        for q, sems in self.qsems.items():
            pend = [(self.sems[s], self.counts[s]) for s in sems
                    if self.counts[s] > 0]
            if pend:
                def qthunk(engine, pend=pend):
                    for sh, v in pend:
                        engine.wait_ge(sh, v)
                self.thunks[q].append(qthunk)

        for eng in ENGINES:
            ths = self.thunks[eng]
            if not ths:
                continue
            dec = getattr(block, eng)

            @dec
            def _(engine, _ths=ths):
                for th in _ths:
                    th(engine)


def build_nc(debug=()):
    nc = bass.Bass("TRN2", target_bir_lowering=False)

    x_in = nc.dram_tensor("x_in", [S, D], F32, kind="ExternalInput").ap()
    wr = nc.dram_tensor("wr", [D, D], F32, kind="ExternalInput").ap()
    wf = nc.dram_tensor("wf", [D, D], F32, kind="ExternalInput").ap()
    wc = nc.dram_tensor("wc", [D, D], F32, kind="ExternalInput").ap()
    wo = nc.dram_tensor("wo", [D, D], F32, kind="ExternalInput").ap()
    lnsc = nc.dram_tensor("lnsc", [128, KT], F32, kind="ExternalInput").ap()
    rw_in = nc.dram_tensor("rw_in", [16, D], F32, kind="ExternalInput").ap()
    rb_in = nc.dram_tensor("rb_in", [16], F32, kind="ExternalInput").ap()
    wg_in = nc.dram_tensor("wg_in", [NE * D, FF], F32, kind="ExternalInput").ap()
    wu_in = nc.dram_tensor("wu_in", [NE * D, FF], F32, kind="ExternalInput").ap()
    wd_in = nc.dram_tensor("wd_in", [NE * FF, D], F32, kind="ExternalInput").ap()
    id_bf = nc.dram_tensor("id_bf", [128, 128], BF16, kind="ExternalInput").ap()
    id_f32 = nc.dram_tensor("id_f32", [128, 128], F32, kind="ExternalInput").ap()
    ones_f32 = nc.dram_tensor("ones_f32", [128, 1], F32, kind="ExternalInput").ap()
    out = nc.dram_tensor("out", [S, D], F32, kind="ExternalOutput").ap()

    dbg = {}
    for name, shape, dt in [
        ("d_xn1T", [D, S], BF16), ("d_h", [D, S], F32), ("d_p", [D, S], BF16),
        ("d_x1", [S, D], F32), ("d_xn2", [D, S], BF16),
        ("d_comb", [16, S], F32), ("d_y", [D, S], F32),
    ]:
        if name in debug:
            dbg[name] = nc.dram_tensor(name, shape, dt, kind="ExternalOutput").ap()

    wat_scr = {w: nc.dram_tensor(f"w{w}_scr", [D, D], BF16, kind="Internal").ap()
               for w in "fcro"}
    wg_scr = nc.dram_tensor("wg_scr", [NE * D, FF], BF16, kind="Internal").ap()
    wu_scr = nc.dram_tensor("wu_scr", [NE * D, FF], BF16, kind="Internal").ap()
    wd_scr = nc.dram_tensor("wd_scr", [NE * FF, D], BF16, kind="Internal").ap()
    x1_scr = nc.dram_tensor("x1_scr", [S, D], F32, kind="Internal").ap()
    xn2_scr = nc.dram_tensor("xn2_scr", [D, S], BF16, kind="Internal").ap()
    row_scr = nc.dram_tensor("row_scr", [1, S], F32, kind="Internal").ap()
    comb_scr = nc.dram_tensor("comb_scr", [16, S], F32, kind="Internal").ap()

    xn2_scr_r = xn2_scr.rearrange("(kt p) t -> p kt t", p=128)

    with ExitStack() as st:
        k = KBuild(nc, st)

        def sb(name, shape, dt, scope=st):
            return scope.enter_context(nc.sbuf_tensor(name, shape, dt))

        idb = sb("idb", [128, 128], BF16)
        idf = sb("idf", [128, 128], F32)
        ones = sb("ones", [128, 1], F32)
        lnsc_sb = sb("lnsc_sb", [128, KT], F32)
        rb_sb = sb("rb_sb", [16, 1], F32)
        STATN = ("sd", "rstd", "nbias")
        stats2 = [{n: sb(f"st{i}_" + n, [128, 1], F32) for n in STATN}
                  for i in range(2)]
        r_stats2 = [{n: Res(f"st{i}_" + n) for n in STATN} for i in range(2)]
        bnb2 = [sb(f"bnb{i}", [128, 4, 6], F32) for i in range(2)]
        ag2 = [sb(f"ag{i}", [128, 2], F32) for i in range(2)]
        r_bnb2 = [Res(f"bnb{i}") for i in range(2)]
        r_ag2 = [Res(f"ag{i}") for i in range(2)]

        ps = [st.enter_context(nc.psum_tensor(f"ps{i}", [128, 512], F32))
              for i in range(6)]
        ps_sq = st.enter_context(nc.psum_tensor("ps_sq", [128, 512], F32))
        ps_bf = st.enter_context(nc.psum_tensor("ps_bf", [128, 1024], BF16))
        r_ps = [Res(f"ps{i}") for i in range(6)]
        r_pssq = Res("ps_sq")
        r_psbf = Res("ps_bf")

        r_idb, r_idf, r_ones = Res("idb"), Res("idf"), Res("ones")
        r_lnsc, r_rb = Res("lnsc"), Res("rb")

        k.op("sync", lambda q: q.dma_start(out=idb[:], in_=id_bf),
             writes=[r_idb], dma=True)
        k.op("sync", lambda q: q.dma_start(out=idf[:], in_=id_f32),
             writes=[r_idf], dma=True)
        k.op("sync", lambda q: q.dma_start(out=ones[:], in_=ones_f32),
             writes=[r_ones], dma=True)
        k.op("sync", lambda q: q.dma_start(out=lnsc_sb[:], in_=lnsc),
             writes=[r_lnsc], dma=True)
        k.op("sync", lambda q: q.dma_start(
            out=rb_sb[:], in_=rb_in.rearrange("(e o) -> e o", o=1)),
            writes=[r_rb], dma=True)

        def d2d(dst, srcap, rows, piece, deps=()):
            toks = []
            for a in range(0, rows, piece):
                b = min(a + piece, rows)
                toks.append(k.op(
                    "gpsimd", lambda g, dst=dst, srcap=srcap, a=a, b=b:
                    g.dma_start(out=dst[a:b, :], in_=srcap[a:b, :]),
                    dma=True, extra_deps=deps))
            return toks

        wmap_in = {"f": wf, "c": wc, "r": wr, "o": wo}
        t_wat = {w: d2d(wat_scr[w], wmap_in[w], D, 256) for w in "fcr"}
        t_wg = [None] * NE
        t_wu = [None] * NE
        t_wd = [None] * NE

        def emit_expert_d2d(e, deps):
            t_wg[e] = d2d(wg_scr[e * D:(e + 1) * D, :],
                          wg_in[e * D:(e + 1) * D, :], D, 512, deps)
            t_wu[e] = d2d(wu_scr[e * D:(e + 1) * D, :],
                          wu_in[e * D:(e + 1) * D, :], D, 512, deps)
            t_wd[e] = d2d(wd_scr[e * FF:(e + 1) * FF, :],
                          wd_in[e * FF:(e + 1) * FF, :], FF, 512, deps)

        def ln_pre(xt, r_xt, si, vscale=None, r_vscale=None):
            bnb, ag = bnb2[si], ag2[si]
            r_bnb, r_ag = r_bnb2[si], r_ag2[si]
            stats, r_stats = stats2[si], r_stats2[si]
            for j in range(4):
                k.op("vector", lambda v, j=j, bnb=bnb, xt=xt: v.bn_stats(
                    out=bnb[:, j, :], in_=xt[:, j * 512:(j + 1) * 512]),
                    reads=[r_xt], writes=[r_bnb])
            k.op("vector", lambda v, ag=ag, bnb=bnb: v.bn_aggr(
                out=ag[:], in_=bnb[:]), reads=[r_bnb], writes=[r_ag])
            if vscale is not None:
                k.op("vector", lambda v, ag=ag, vscale=vscale: v.tensor_scalar(
                    out=ag[:, 1:2], in0=ag[:, 1:2], scalar1=vscale,
                    scalar2=EPS, op0=OP.mult, op1=OP.add),
                    reads=[r_ag, r_vscale], writes=[r_ag])
            else:
                k.op("vector", lambda v, ag=ag: v.tensor_scalar(
                    out=ag[:, 1:2], in0=ag[:, 1:2], scalar1=EPS, scalar2=None,
                    op0=OP.add), reads=[r_ag], writes=[r_ag])
            k.op("scalar", lambda s, ag=ag, sd=stats["sd"]: s.activation(
                out=sd[:, 0:1], in_=ag[:, 1:2], func=AF.Sqrt),
                reads=[r_ag], writes=[r_stats["sd"]])

        def ln_post(si, mscale=None, r_mscale=None):
            stats, r_stats = stats2[si], r_stats2[si]
            ag, r_ag = ag2[si], r_ag2[si]
            k.op("vector", lambda v, stats=stats: v.reciprocal(
                stats["rstd"][:, 0:1], stats["sd"][:, 0:1]),
                reads=[r_stats["sd"]], writes=[r_stats["rstd"]])
            if mscale is not None:
                k.op("vector", lambda v, stats=stats, mscale=mscale:
                     v.tensor_tensor(
                         out=stats["rstd"][:, 0:1], in0=stats["rstd"][:, 0:1],
                         in1=mscale, op=OP.mult),
                     reads=[r_stats["rstd"], r_mscale],
                     writes=[r_stats["rstd"]])
            k.op("vector", lambda v, stats=stats, ag=ag: v.tensor_scalar(
                out=stats["nbias"][:, 0:1], in0=ag[:, 0:1],
                scalar1=stats["rstd"][:, 0:1], scalar2=-1.0,
                op0=OP.mult, op1=OP.mult),
                reads=[r_ag, r_stats["rstd"]], writes=[r_stats["nbias"]])

        def ln_stats(xt, r_xt, dump, r_dump, si=0):
            ln_pre(xt, r_xt, si)
            ln_post(si)

        def ln_apply(src, r_src, dst, r_dst, si=0):
            stats = stats2[si]
            r_stats = r_stats2[si]
            k.op("vector", lambda v: v.tensor_scalar(
                out=dst, in0=src, scalar1=stats["rstd"][:, 0:1],
                scalar2=stats["nbias"][:, 0:1], op0=OP.mult, op1=OP.add),
                reads=[r_src, r_stats["rstd"], r_stats["nbias"]],
                writes=[r_dst])

        fenceA = k.fence()
        scA = st.enter_context(ExitStack())
        p_bf = sb("p_bf", [128, KT, S], BF16, scA)
        scAx = st.enter_context(ExitStack())
        x_n1T = sb("x_n1T", [128, KT, S], BF16, scAx)
        r_xn1T = [Res(f"x_n1T{t}", fenceA) for t in range(16)]
        r_pbf = [Res(f"p{m}", fenceA) for m in range(KT)]
        r_dump = Res("dump", fenceA)

        scA1 = st.enter_context(ExitStack())
        xtile = [sb(f"xtile{i}", [128, D], F32, scA1) for i in range(4)]
        xn1t = [sb(f"xn1t{i}", [128, D], BF16, scA1) for i in range(2)]
        r_xtile = [Res(f"xtile{i}", fenceA) for i in range(4)]
        r_xn1t = [Res(f"xn1t{i}", fenceA) for i in range(2)]

        def a1_load(tt):
            b4 = tt % 4
            k.op("sync", lambda q, tt=tt, xb=xtile[b4]: q.dma_start(
                out=xb[:], in_=x_in[tt * 128:(tt + 1) * 128, :]),
                writes=[r_xtile[b4]], dma=True)

        def a1_tiles(tp):
            for i in range(2):
                if tp + 2 + i < 16:
                    a1_load(tp + 2 + i)
            for i in range(2):
                b4 = (tp + i) % 4
                ln_pre(xtile[b4], r_xtile[b4], i)
            for i in range(2):
                ln_post(i)
            for i in range(2):
                b4 = (tp + i) % 4
                ln_apply(xtile[b4][:], r_xtile[b4], xn1t[i][:], r_xn1t[i], i)
            for i in range(2):
                tt = tp + i
                for half in range(2):
                    def tr(t, tt=tt, half=half, x1t=xn1t[i]):
                        last = None
                        for j in range(8):
                            kt = half * 8 + j
                            last = t.transpose(
                                out=ps_bf[:, j * 128:(j + 1) * 128],
                                in_=x1t[:, kt * 128:(kt + 1) * 128],
                                identity=idb[:])
                        return last
                    k.op("tensor", tr, reads=[r_xn1t[i], r_idb],
                         writes=[r_psbf])
                    k.op("scalar", lambda s, tt=tt, half=half: s.copy(
                        out=x_n1T[:, half * 8:(half + 1) * 8,
                                  tt * 128:(tt + 1) * 128],
                        in_=ps_bf.ap().rearrange("p (b c) -> p b c", c=128)),
                        reads=[r_psbf], writes=[r_xn1T[tt]])

        a1_load(0)
        a1_load(1)
        for tp in range(0, 16, 2):
            a1_tiles(tp)

        if "d_xn1T" in dbg:
            k.op("sync", lambda q: q.dma_start(
                out=dbg["d_xn1T"].rearrange("(kt p) t -> p kt t", p=128),
                in_=x_n1T[:]), reads=r_xn1T, dma=True)

        scA1.close()
        scA2 = st.enter_context(ExitStack())
        f0b = [sb(f"f0_{i}", [128, S], F32, scA2) for i in range(2)]
        c0b = [sb(f"c0_{i}", [128, S], F32, scA2) for i in range(2)]
        lvf = sb("lvf", [128, S], F32, scA2)
        lvc = sb("lvc", [128, S], F32, scA2)
        Hbuf = sb("Hbuf", [128, S], F32, scA2)
        tmpA = sb("tmpA", [128, 512], F32, scA2)
        sqacc = sb("sqacc", [128, 512], F32, scA2)
        sqacc3 = sb("sqacc3", [1, 512], F32, scA2)
        pan = {w: [sb(f"pan_{w}{i}", [128, KT * 128], BF16, scA2)
                   for i in range(2 if w == "r" else 1)] for w in "fcr"}
        r_f0b = [Res(f"f0_{i}", fenceA) for i in range(2)]
        r_c0b = [Res(f"c0_{i}", fenceA) for i in range(2)]
        r_lvf = Res("lvf", fenceA)
        r_lvc = Res("lvc", fenceA)
        r_H = Res("H", fenceA)
        r_tmpA = Res("tmpA", fenceA)
        r_sqacc = Res("sqacc", fenceA)
        r_sqacc3 = Res("sqacc3", fenceA)
        r_pan = {w: [Res(f"pan_{w}{i}", fenceA)
                     for i in range(2 if w == "r" else 1)] for w in "fcr"}



        def emit_scan(M, f0, c0, r_f0, r_c0):
            offs = {}
            off = 0
            n = S
            lvl = 0
            prev_f, prev_c = f0, c0
            prev_rf, prev_rc = r_f0, r_c0
            while n >= 2:
                n2 = n // 2
                lvl += 1
                offs[lvl] = (off, n2)
                cf = lvf[:, off:off + n2]
                cc = lvc[:, off:off + n2]
                pf, pc = prev_f, prev_c
                if lvl == 1:
                    af = pf[:, 0:S:2]; bf_ = pf[:, 1:S:2]
                    ac = pc[:, 0:S:2]; bc_ = pc[:, 1:S:2]
                else:
                    po, pn = offs[lvl - 1]
                    af = lvf[:, po:po + pn:2]; bf_ = lvf[:, po + 1:po + pn:2]
                    ac = lvc[:, po:po + pn:2]; bc_ = lvc[:, po + 1:po + pn:2]
                for q0 in range(0, n2, 512):
                    q1 = min(q0 + 512, n2)
                    w_ = q1 - q0
                    k.op("vector", lambda v, ac=ac, bc_=bc_, q0=q0, q1=q1, w_=w_:
                         v.tensor_tensor(
                             out=tmpA[:, 0:w_], in0=ac[:, q0:q1],
                             in1=bc_[:, q0:q1], op=OP.subtract),
                         reads=[prev_rf, prev_rc], writes=[r_tmpA])
                    k.op("vector", lambda v, bf_=bf_, q0=q0, q1=q1, w_=w_:
                         v.tensor_tensor(
                             out=tmpA[:, 0:w_], in0=tmpA[:, 0:w_],
                             in1=bf_[:, q0:q1], op=OP.mult),
                         reads=[r_tmpA, prev_rf], writes=[r_tmpA])
                    k.op("vector", lambda v, cc=cc, bc_=bc_, q0=q0, q1=q1, w_=w_:
                         v.tensor_tensor(
                             out=cc[:, q0:q1], in0=tmpA[:, 0:w_],
                             in1=bc_[:, q0:q1], op=OP.add),
                         reads=[r_tmpA, prev_rc], writes=[r_lvc])
                k.op("vector", lambda v, cf=cf, af=af, bf_=bf_: v.tensor_tensor(
                    out=cf, in0=af, in1=bf_, op=OP.mult),
                    reads=[prev_rf], writes=[r_lvf])
                prev_f = prev_c = None
                prev_rf, prev_rc = r_lvf, r_lvc
                off += n2
                n = n2
            for lvl in range(11, -1, -1):
                step = 1 << lvl
                if lvl == 0:
                    e_f, e_c, nl = f0, c0, S
                    r_ef, r_ec = r_f0, r_c0
                else:
                    o, nl = offs[lvl]
                    e_f = lvf[:, o:o + nl]
                    e_c = lvc[:, o:o + nl]
                    r_ef, r_ec = r_lvf, r_lvc
                k.op("vector", lambda v, e_c=e_c, step=step: v.tensor_copy(
                    out=Hbuf[:, step - 1:step], in_=e_c[:, 0:1]),
                    reads=[r_ec], writes=[r_H])
                ncomb = nl // 2 - 1
                if ncomb <= 0:
                    continue
                cs = Hbuf[:, 2 * step - 1: 2 * step - 1 + (ncomb - 1) * 2 * step + 1: 2 * step]
                fe = e_f[:, 2:2 + (ncomb - 1) * 2 + 1:2]
                ce = e_c[:, 2:2 + (ncomb - 1) * 2 + 1:2]
                hw_ = Hbuf[:, 3 * step - 1: 3 * step - 1 + (ncomb - 1) * 2 * step + 1: 2 * step]
                for q0 in range(0, ncomb, 512):
                    q1 = min(q0 + 512, ncomb)
                    w_ = q1 - q0
                    k.op("vector", lambda v, cs=cs, ce=ce, q0=q0, q1=q1, w_=w_:
                         v.tensor_tensor(
                             out=tmpA[:, 0:w_], in0=cs[:, q0:q1],
                             in1=ce[:, q0:q1], op=OP.subtract),
                         reads=[r_H, r_ec], writes=[r_tmpA])
                    k.op("vector", lambda v, fe=fe, q0=q0, q1=q1, w_=w_:
                         v.tensor_tensor(
                             out=tmpA[:, 0:w_], in0=tmpA[:, 0:w_],
                             in1=fe[:, q0:q1], op=OP.mult),
                         reads=[r_tmpA, r_ef], writes=[r_tmpA])
                    k.op("vector", lambda v, hw_=hw_, ce=ce, q0=q0, q1=q1, w_=w_:
                         v.tensor_tensor(
                             out=hw_[:, q0:q1], in0=tmpA[:, 0:w_],
                             in1=ce[:, q0:q1], op=OP.add),
                         reads=[r_tmpA, r_ec], writes=[r_H])

        for M in range(KT):
            f0, c0 = f0b[M % 2], c0b[M % 2]
            r_f0, r_c0 = r_f0b[M % 2], r_c0b[M % 2]
            for wi, wn in enumerate("fcr"):
                mpw = M % len(pan[wn])
                k.op("sync", lambda q, wn=wn, M=M, mpw=mpw: q.dma_start(
                    out=pan[wn][mpw][:],
                    in_=wat_scr[wn][M * 128:(M + 1) * 128, :]),
                    extra_deps=[t_wat[wn][M // 2]], writes=[r_pan[wn][mpw]],
                    dma=True)
            for g in range(4):
                gsl = slice(g * 512, (g + 1) * 512)
                xr = [r_xn1T[g * 4 + j] for j in range(4)]
                pF = ps[g % 2]; rF = r_ps[g % 2]
                pC = ps[2 + g % 2]; rC = r_ps[2 + g % 2]
                pR = ps[4 + g % 2]; rR = r_ps[4 + g % 2]
                for wn, pp, rp in (("f", pF, rF), ("c", pC, rC), ("r", pR, rR)):
                    mpw = M % len(pan[wn])

                    def mm(t, wn=wn, pp=pp, g=g, mpw=mpw):
                        last = None
                        for kt in range(KT):
                            last = t.matmul(
                                out=pp[:],
                                lhsT=pan[wn][mpw][:, kt * 128:(kt + 1) * 128],
                                rhs=x_n1T[:, kt, g * 512:(g + 1) * 512],
                                start=(kt == 0), stop=(kt == KT - 1))
                        return last
                    mmt = k.op("tensor", mm, reads=[r_pan[wn][mpw]] + xr,
                               writes=[rp])
                if g == 0 and M == 0:
                    t_wat["o"] = d2d(wat_scr["o"], wmap_in["o"], D, 256,
                                     deps=[mmt])
                if g == 0 and M < NE:
                    emit_expert_d2d(M, [mmt])
                k.op("scalar", lambda s, pF=pF, f0=f0, gsl=gsl: s.activation(
                    out=f0[:, gsl], in_=pF[:], func=AF.Tanh, scale=0.5),
                    reads=[rF], writes=[r_f0])
                k.op("vector", lambda v, f0=f0, gsl=gsl: v.tensor_scalar(
                    out=f0[:, gsl], in0=f0[:, gsl], scalar1=0.5, scalar2=0.5,
                    op0=OP.mult, op1=OP.add),
                    reads=[r_f0], writes=[r_f0])
                k.op("scalar", lambda s, pC=pC, c0=c0, gsl=gsl: s.activation(
                    out=c0[:, gsl], in_=pC[:], func=AF.Silu),
                    reads=[rC], writes=[r_c0])
                k.op("scalar", lambda s, pR=pR, M=M, gsl=gsl: s.activation(
                    out=p_bf[:, M, gsl], in_=pR[:], func=AF.Silu),
                    reads=[rR], writes=[r_pbf[M]])
            def emit_sq_square(Mprev):
                k.op("scalar", lambda s: s.activation(
                    out=Hbuf[:], in_=Hbuf[:], func=AF.Square),
                    reads=[r_H], writes=[r_H])

            def emit_sq_slab(Mprev, j):
                dst = (sqacc3[0:1, :] if j == 3
                       else sqacc[j * 32:j * 32 + 1, :])
                rdst = r_sqacc3 if j == 3 else r_sqacc
                k.op("tensor", lambda t, j=j: t.matmul(
                    out=ps_sq[0:1, :], lhsT=ones[:],
                    rhs=Hbuf[:, j * 512:(j + 1) * 512],
                    start=True, stop=True),
                    reads=[r_H, r_ones], writes=[r_pssq])
                if Mprev == 0:
                    k.op("vector", lambda v, dst=dst: v.tensor_copy(
                        out=dst, in_=ps_sq[0:1, :]),
                        reads=[r_pssq], writes=[rdst])
                else:
                    k.op("vector", lambda v, dst=dst: v.tensor_tensor(
                        out=dst, in0=dst, in1=ps_sq[0:1, :], op=OP.add),
                        reads=[r_pssq, rdst], writes=[rdst])

            def emit_sq(Mprev):
                emit_sq_square(Mprev)
                for j in range(4):
                    emit_sq_slab(Mprev, j)

            if M > 0:
                emit_sq(M - 1)
            emit_scan(M, f0, c0, r_f0, r_c0)
            if "d_h" in dbg:
                k.op("sync", lambda q, M=M: q.dma_start(
                    out=dbg["d_h"][M * 128:(M + 1) * 128, :], in_=Hbuf[:]),
                    reads=[r_H], dma=True)
            k.op("vector", lambda v, M=M: v.scalar_tensor_tensor(
                out=p_bf[:, M, :], in0=Hbuf[:], scalar=lnsc_sb[:, M:M + 1],
                in1=p_bf[:, M, :], op0=OP.mult, op1=OP.mult),
                reads=[r_H, r_lnsc, r_pbf[M]], writes=[r_pbf[M]])

        if "d_p" in dbg:
            k.op("sync", lambda q: q.dma_start(
                out=dbg["d_p"].rearrange("(kt p) t -> p kt t", p=128),
                in_=p_bf[:]), reads=r_pbf, dma=True)

        emit_sq(KT - 1)
        emit_expert_d2d(16, [k.barrier_token_tensor()])

        r_rowscr = Res("row_scr")
        for j in range(4):
            srcx = sqacc3[0:1, :] if j == 3 else sqacc[j * 32:j * 32 + 1, :]
            k.op("sync", lambda q, j=j, srcx=srcx: q.dma_start(
                out=row_scr[0:1, j * 512:(j + 1) * 512], in_=srcx),
                reads=[r_sqacc3 if j == 3 else r_sqacc],
                writes=[r_rowscr], dma=True)

        scA2.close()
        scAx.close()

        fence3 = k.fence()
        scA3 = st.enter_context(ExitStack())
        y2s = sb("y2s", [128, KT, 512], F32, scA3)
        craw = [sb(f"craw{i}", [128, 1], F32, scA3) for i in range(2)]
        csqb = [sb(f"csq{i}", [128, 1], F32, scA3) for i in range(2)]
        ccol = [sb(f"ccol{i}", [128, 1], F32, scA3) for i in range(2)]
        r_craw = [Res(f"craw{i}", fence3) for i in range(2)]
        r_csq = [Res(f"csq{i}", fence3) for i in range(2)]
        r_ccol = [Res(f"ccol{i}", fence3) for i in range(2)]
        y2T = [sb(f"y2T{i}", [128, D], F32, scA3) for i in range(2)]
        xrt = [sb(f"xrt{i}", [128, D], F32, scA3) for i in range(2)]
        x1b = [sb(f"x1b{i}", [128, D], F32, scA3) for i in range(2)]
        xn2b = [sb(f"xn2b{i}", [128, D], BF16, scA3) for i in range(2)]
        xn2st = [sb(f"xn2st{i}", [128, KT, 128], BF16, scA3) for i in range(2)]
        dump3 = [sb(f"dump3{i}", [128, D], F32, scA3) for i in range(2)]
        pano = [sb(f"pano{i}", [128, KT * 128], BF16, scA3) for i in range(2)]
        r_y2s = [Res(f"y2s{m}", fence3) for m in range(KT)]
        r_y2T = [Res(f"y2T{i}", fence3) for i in range(2)]
        r_xrt = [Res(f"xrt{i}", fence3) for i in range(2)]
        r_x1b = [Res(f"x1b{i}", fence3) for i in range(2)]
        r_xn2b = [Res(f"xn2b{i}", fence3) for i in range(2)]
        r_xn2st = [Res(f"xn2st{i}", fence3) for i in range(2)]
        r_dump3 = [Res(f"dump3{i}", fence3) for i in range(2)]
        r_pano = [Res(f"pano{i}", fence3) for i in range(2)]
        r_x1scr = [Res(f"x1_scr{t}") for t in range(16)]
        r_xn2scr = [Res(f"xn2_scr{t}") for t in range(16)]

        def c_chain(tt, i):
            k.op("sync", lambda q, tt=tt, i=i: q.dma_start(
                out=craw[i][:, 0:1],
                in_=row_scr[0:1, tt * 128:(tt + 1) * 128]
                .rearrange("a (p o) -> (a p) o", o=1)),
                reads=[r_rowscr], writes=[r_craw[i]], dma=True)
            k.op("vector", lambda v, i=i: v.tensor_scalar(
                out=csqb[i][:, 0:1], in0=craw[i][:, 0:1], scalar1=1.0 / D,
                scalar2=EPS, op0=OP.mult, op1=OP.add),
                reads=[r_craw[i]], writes=[r_csq[i]])
            k.op("vector", lambda v, i=i: v.reciprocal(
                csqb[i][:, 0:1], csqb[i][:, 0:1]),
                reads=[r_csq[i]], writes=[r_csq[i]])
            k.op("scalar", lambda s, i=i: s.activation(
                out=ccol[i][:, 0:1], in_=csqb[i][:, 0:1], func=AF.Sqrt),
                reads=[r_csq[i]], writes=[r_ccol[i]])

        for sl in range(4):
            ssl = slice(sl * 512, (sl + 1) * 512)
            for M in range(KT):
                pb = M % 2
                k.op("sync", lambda q, M=M: q.dma_start(
                    out=pano[M % 2][:],
                    in_=wat_scr["o"][M * 128:(M + 1) * 128, :]),
                    extra_deps=[t_wat["o"][M // 2]], writes=[r_pano[M % 2]],
                    dma=True)
                pp, rp = ps[pb], r_ps[pb]
                lastw = r_pbf[KT - 1].writer
                lw = (k.sems[lastw.sem], lastw.val) if lastw is not None else None

                def mmo(t, M=M, pb=pb, sl=sl, lw=lw):
                    last = None
                    for kt in range(KT):
                        if kt == KT - 1 and lw is not None:
                            t.wait_ge(lw[0], lw[1])
                        last = t.matmul(
                            out=ps[pb][:],
                            lhsT=pano[M % 2][:, kt * 128:(kt + 1) * 128],
                            rhs=p_bf[:, kt, sl * 512:(sl + 1) * 512],
                            start=(kt == 0), stop=(kt == KT - 1))
                    return last
                tok_mmo = k.op("tensor", mmo,
                               reads=[r_pano[M % 2]] + r_pbf[0:KT - 1],
                               writes=[rp])
                r_pbf[KT - 1].readers.append(tok_mmo)
                k.op("scalar", lambda s, M=M, pb=pb: s.copy(
                    out=y2s[:, M, :], in_=ps[pb][:]),
                    reads=[rp], writes=[r_y2s[M]])
            for tb0 in range(0, 4, 2):
                tts = (sl * 4 + tb0, sl * 4 + tb0 + 1)
                for i in range(2):
                    k.op("sync", lambda q, tt=tts[i], xb=xrt[i]: q.dma_start(
                        out=xb[:], in_=x_in[tt * 128:(tt + 1) * 128, :]),
                        writes=[r_xrt[i]], dma=True)
                    c_chain(tts[i], i)
                for i in range(2):
                    tb = tb0 + i
                    for qd in range(4):
                        pb2 = 2 + qd % 2

                        def trq(t, qd=qd, tb=tb, pb2=pb2):
                            last = None
                            for j in range(4):
                                M = qd * 4 + j
                                last = t.transpose(
                                    out=ps[pb2][:, j * 128:(j + 1) * 128],
                                    in_=y2s[:, M, tb * 128:(tb + 1) * 128],
                                    identity=idf[:])
                            return last
                        k.op("tensor", trq,
                             reads=[r_y2s[qd * 4 + j] for j in range(4)]
                             + [r_idf], writes=[r_ps[pb2]])
                        k.op("scalar", lambda s, qd=qd, pb2=pb2, yT=y2T[i]:
                             s.copy(out=yT[:, qd * 512:(qd + 1) * 512],
                                    in_=ps[pb2][:]),
                             reads=[r_ps[pb2]], writes=[r_y2T[i]])
                for i in range(2):
                    ln_pre(y2T[i], r_y2T[i], i,
                           vscale=csqb[i][:, 0:1], r_vscale=r_csq[i])
                for i in range(2):
                    ln_post(i, mscale=ccol[i][:, 0:1], r_mscale=r_ccol[i])
                for i in range(2):
                    ln_apply(y2T[i][:], r_y2T[i], dump3[i][:], r_dump3[i], i)
                for i in range(2):
                    k.op("vector", lambda v, xb=xrt[i], dp=dump3[i],
                         xob=x1b[i]: v.tensor_tensor(
                             out=xob[:], in0=dp[:], in1=xb[:], op=OP.add),
                         reads=[r_dump3[i], r_xrt[i]], writes=[r_x1b[i]])
                    k.op("sync", lambda q, tt=tts[i], xob=x1b[i]: q.dma_start(
                        out=x1_scr[tt * 128:(tt + 1) * 128, :], in_=xob[:]),
                        reads=[r_x1b[i]], writes=[r_x1scr[tts[i]]], dma=True)
                for i in range(2):
                    ln_pre(x1b[i], r_x1b[i], i)
                for i in range(2):
                    ln_post(i)
                for i in range(2):
                    ln_apply(x1b[i][:], r_x1b[i], xn2b[i][:], r_xn2b[i], i)
                for i in range(2):
                    for half in range(2):
                        def trx(t, half=half, nb=xn2b[i]):
                            last = None
                            for j in range(8):
                                kt = half * 8 + j
                                last = t.transpose(
                                    out=ps_bf[:, j * 128:(j + 1) * 128],
                                    in_=nb[:, kt * 128:(kt + 1) * 128],
                                    identity=idb[:])
                            return last
                        k.op("tensor", trx, reads=[r_xn2b[i], r_idb],
                             writes=[r_psbf])
                        k.op("scalar", lambda s, half=half, nst=xn2st[i]:
                             s.copy(out=nst[:, half * 8:(half + 1) * 8, :],
                                    in_=ps_bf.ap()
                                    .rearrange("p (b c) -> p b c", c=128)),
                             reads=[r_psbf], writes=[r_xn2st[i]])
                    k.op("sync", lambda q, tt=tts[i], nst=xn2st[i]:
                         q.dma_start(
                             out=xn2_scr_r[:, :, tt * 128:(tt + 1) * 128],
                             in_=nst[:]),
                         reads=[r_xn2st[i]], writes=[r_xn2scr[tts[i]]],
                         dma=True)

        if "d_x1" in dbg:
            k.op("sync", lambda q: q.dma_start(out=dbg["d_x1"], in_=x1_scr),
                 reads=r_x1scr, dma=True)
        if "d_xn2" in dbg:
            k.op("sync", lambda q: q.dma_start(out=dbg["d_xn2"], in_=xn2_scr),
                 reads=r_xn2scr, dma=True)

        scA3.close()
        scA.close()

        fenceR = k.fence()
        scR = st.enter_context(ExitStack())
        xsl = sb("xsl", [128, KT, 512], BF16, scR)
        rwp = sb("rwp", [128, D], BF16, scR)
        rT = sb("rT", [128, KT * 16], BF16, scR)
        sco = sb("sco", [128, S], F32, scR)
        cho = sb("cho", [128, S], F32, scR)
        scot = sb("scot", [128, 16], F32, scR)
        chot = sb("chot", [128, 16], F32, scR)
        mx8 = sb("mx8", [128, 8], F32, scR)
        msk = sb("msk", [128, 16], F32, scR)
        cmb = sb("cmb", [128, 16], F32, scR)
        den = sb("den", [128, 1], F32, scR)
        ccm = sb("ccm", [16, S], F32, scR)
        rwstg = sb("rwstg", [16, D], F32, scR)
        r_xsl = Res("xsl", fenceR)
        r_rwp = Res("rwp", fenceR)
        r_rT = Res("rT", fenceR)
        r_sco = Res("sco", fenceR)
        r_cho = Res("cho", fenceR)
        r_scot = Res("scot", fenceR)
        r_chot = Res("chot", fenceR)
        r_mx8 = Res("mx8", fenceR)
        r_msk = Res("msk", fenceR)
        r_cmb = Res("cmb", fenceR)
        r_den = Res("den", fenceR)
        r_ccm = Res("ccm", fenceR)
        r_rwstg = Res("rwstg", fenceR)
        r_combscr = Res("comb_scr")

        k.op("vector", lambda v: v.memset(rwp[:], 0.0), writes=[r_rwp])
        k.op("vector", lambda v: v.memset(sco[:], 0.0), writes=[r_sco])
        k.op("vector", lambda v: v.memset(cho[:], 0.0), writes=[r_cho])
        k.op("sync", lambda q: q.dma_start(out=rwstg[:], in_=rw_in),
             writes=[r_rwstg], dma=True)
        k.op("scalar", lambda s: s.copy(out=rwp[0:16, :], in_=rwstg[:]),
             reads=[r_rwstg], writes=[r_rwp])
        for kt in range(KT):
            k.op("tensor", lambda t, kt=kt: t.transpose(
                out=ps_bf[:, 0:128], in_=rwp[:, kt * 128:(kt + 1) * 128],
                identity=idb[:]), reads=[r_rwp, r_idb], writes=[r_psbf])
            k.op("scalar", lambda s, kt=kt: s.copy(
                out=rT[:, kt * 16:(kt + 1) * 16], in_=ps_bf[:, 0:16]),
                reads=[r_psbf], writes=[r_rT])
        for sl in range(4):
            k.op("sync", lambda q, sl=sl: q.dma_start(
                out=xsl[:], in_=xn2_scr_r[:, :, sl * 512:(sl + 1) * 512]),
                reads=r_xn2scr, writes=[r_xsl], dma=True)

            def mml(t, sl=sl):
                last = None
                for kt in range(KT):
                    last = t.matmul(
                        out=ps[4][0:16, :], lhsT=rT[:, kt * 16:(kt + 1) * 16],
                        rhs=xsl[:, kt, :], start=(kt == 0), stop=(kt == KT - 1))
                return last
            k.op("tensor", mml, reads=[r_rT, r_xsl], writes=[r_ps[4]])
            k.op("scalar", lambda s, sl=sl: s.activation(
                out=sco[0:16, sl * 512:(sl + 1) * 512], in_=ps[4][0:16, :],
                func=AF.Sigmoid), reads=[r_ps[4]], writes=[r_sco])
            k.op("vector", lambda v, sl=sl: v.tensor_scalar(
                out=cho[0:16, sl * 512:(sl + 1) * 512],
                in0=sco[0:16, sl * 512:(sl + 1) * 512],
                scalar1=rb_sb[0:16, 0:1], scalar2=None, op0=OP.add),
                reads=[r_sco, r_rb], writes=[r_cho])
        for tt in range(16):
            tsl = slice(tt * 128, (tt + 1) * 128)
            k.op("tensor", lambda t, tsl=tsl: t.transpose(
                out=ps[2][:, 0:128], in_=sco[:, tsl], identity=idf[:]),
                reads=[r_sco, r_idf], writes=[r_ps[2]])
            k.op("vector", lambda v: v.tensor_copy(
                out=scot[:], in_=ps[2][:, 0:16]),
                reads=[r_ps[2]], writes=[r_scot])
            k.op("tensor", lambda t, tsl=tsl: t.transpose(
                out=ps[3][:, 0:128], in_=cho[:, tsl], identity=idf[:]),
                reads=[r_cho, r_idf], writes=[r_ps[3]])
            k.op("vector", lambda v: v.tensor_copy(
                out=chot[:], in_=ps[3][:, 0:16]),
                reads=[r_ps[3]], writes=[r_chot])
            k.op("vector", lambda v: v.max(out=mx8[:], in_=chot[:]),
                 reads=[r_chot], writes=[r_mx8])
            k.op("vector", lambda v: v.tensor_scalar(
                out=msk[:], in0=chot[:], scalar1=mx8[:, 3:4], scalar2=None,
                op0=OP.is_ge), reads=[r_chot, r_mx8], writes=[r_msk])
            k.op("vector", lambda v: v.tensor_tensor(
                out=cmb[:], in0=scot[:], in1=msk[:], op=OP.mult),
                reads=[r_scot, r_msk], writes=[r_cmb])
            k.op("vector", lambda v: v.reduce_sum(
                den[:, 0:1], cmb[:], axis=AX.X),
                reads=[r_cmb], writes=[r_den])
            k.op("vector", lambda v: v.tensor_scalar(
                out=den[:, 0:1], in0=den[:, 0:1], scalar1=1e-20, scalar2=None,
                op0=OP.add), reads=[r_den], writes=[r_den])
            k.op("vector", lambda v: v.reciprocal(den[:, 0:1], den[:, 0:1]),
                 reads=[r_den], writes=[r_den])
            k.op("vector", lambda v: v.tensor_scalar(
                out=cmb[:], in0=cmb[:], scalar1=den[:, 0:1], scalar2=None,
                op0=OP.mult), reads=[r_cmb, r_den], writes=[r_cmb])
            k.op("tensor", lambda t: t.transpose(
                out=ps[5][0:16, 0:128], in_=cmb[:], identity=idf[:]),
                reads=[r_cmb, r_idf], writes=[r_ps[5]])
            k.op("vector", lambda v, tsl=tsl: v.tensor_copy(
                out=ccm[:, tsl], in_=ps[5][0:16, 0:128]),
                reads=[r_ps[5]], writes=[r_ccm])
        k.op("sync", lambda q: q.dma_start(out=comb_scr, in_=ccm[:]),
             reads=[r_ccm], writes=[r_combscr], dma=True)
        if "d_comb" in dbg:
            k.op("sync", lambda q: q.dma_start(out=dbg["d_comb"], in_=ccm[:]),
                 reads=[r_ccm], dma=True)

        scR.close()

        fenceC = k.fence()
        scC = st.enter_context(ExitStack())
        y_acc = sb("y_acc", [128, KT, CHUNK], F32, scC)
        slot = [sb(f"slot{i}", [128, KT * FF], BF16, scC) for i in range(2)]
        r_yacc = [Res(f"yacc{m}", fenceC) for m in range(KT)]
        r_slot = [[Res(f"slot{i}_{j}", fenceC) for j in range(2)]
                  for i in range(2)]

        def load_T(seq, e, which, t_w):
            si = seq % 2
            if which in ("g", "u"):
                scr = wg_scr if which == "g" else wu_scr
                for half in range(2):
                    queue = "sync" if half == 0 else "scalar"
                    k.op(queue, lambda q, e=e, half=half, si=si, scr=scr:
                         q.dma_start(
                             out=slot[si].ap()
                             .rearrange("p (kt f) -> p kt f", f=FF)
                             [:, half * 8:(half + 1) * 8, :],
                             in_=scr[e * D:(e + 1) * D, :]
                             .rearrange("(kt p) f -> p kt f", p=128)
                             [:, half * 8:(half + 1) * 8, :]),
                         extra_deps=t_w[half * 2:half * 2 + 2],
                         writes=[r_slot[si][half]], dma=True)
            else:
                for half in range(2):
                    queue = "sync" if half == 0 else "scalar"
                    k.op(queue, lambda q, e=e, half=half, si=si:
                         q.dma_start(
                             out=slot[si].ap()
                             .rearrange("p (fk h) -> p fk h", h=D)
                             [:, half * 4:(half + 1) * 4, :],
                             in_=wd_scr[e * FF:(e + 1) * FF, :]
                             .rearrange("(fk p) h -> p fk h", p=128)
                             [:, half * 4:(half + 1) * 4, :]),
                         extra_deps=t_w[half:half + 1],
                         writes=[r_slot[si][half]], dma=True)
            return si

        all_out_toks = []

        def emit_chunk(ch):
            c0_ = ch * CHUNK
            scE = st.enter_context(ExitStack())
            silu_g = sb(f"silu_g{ch}", [128, FT, CHUNK], BF16, scE)
            a_bf = sb(f"a_bf{ch}", [128, FT, CHUNK], BF16, scE)
            xs = sb(f"xs{ch}", [128, KT, CHUNK], BF16, scE)
            comb_bc = sb(f"comb_bc{ch}", [128, CHUNK], F32, scE)
            u512 = sb(f"u512_{ch}", [128, 512], F32, scE)
            fE = k.fence()
            r_silu = [Res(f"silu{ft}", fE) for ft in range(FT)]
            r_abf = [Res(f"a{ft}", fE) for ft in range(FT)]
            r_xs = Res("xs", fE)
            r_cbc = Res("comb_bc", fE)
            r_u512 = Res("u512", fE)
            k.op("sync", lambda q, c0_=c0_, xs=xs: q.dma_start(
                out=xs[:], in_=xn2_scr_r[:, :, c0_:c0_ + CHUNK]),
                reads=r_xn2scr, writes=[r_xs], dma=True)

            seq = 0
            for e in range(NE):
                si_g = load_T(seq, e, "g", t_wg[e]); seq += 1
                si_u = load_T(seq, e, "u", t_wu[e]); seq += 1
                if e < 16:
                    k.op("sync", lambda q, e=e, c0_=c0_: q.dma_start(
                        out=comb_bc[:],
                        in_=comb_scr[e:e + 1, c0_:c0_ + CHUNK]
                        .partition_broadcast(128)),
                        reads=[r_combscr], writes=[r_cbc], dma=True)
                for ft in range(FT):
                    for sl2 in range(CHUNK // 512):
                        pi = (ft * 2 + sl2) % 2

                        def mmg(t, ft=ft, sl2=sl2, pi=pi, si_g=si_g):
                            last = None
                            for kt in range(KT):
                                last = t.matmul(
                                    out=ps[pi][:],
                                    lhsT=slot[si_g][:, kt * FF + ft * 128:
                                                    kt * FF + (ft + 1) * 128],
                                    rhs=xs[:, kt, sl2 * 512:(sl2 + 1) * 512],
                                    start=(kt == 0), stop=(kt == KT - 1))
                            return last
                        k.op("tensor", mmg, reads=r_slot[si_g] + [r_xs],
                             writes=[r_ps[pi]])
                        k.op("scalar", lambda s, ft=ft, sl2=sl2, pi=pi:
                             s.activation(
                                 out=silu_g[:, ft, sl2 * 512:(sl2 + 1) * 512],
                                 in_=ps[pi][:], func=AF.Silu),
                             reads=[r_ps[pi]], writes=[r_silu[ft]])
                si_d = load_T(seq, e, "d", t_wd[e]); seq += 1
                for ft in range(FT):
                    for sl2 in range(CHUNK // 512):
                        pi = 2 + (ft * 2 + sl2) % 2

                        def mmu(t, ft=ft, sl2=sl2, pi=pi, si_u=si_u):
                            last = None
                            for kt in range(KT):
                                last = t.matmul(
                                    out=ps[pi][:],
                                    lhsT=slot[si_u][:, kt * FF + ft * 128:
                                                    kt * FF + (ft + 1) * 128],
                                    rhs=xs[:, kt, sl2 * 512:(sl2 + 1) * 512],
                                    start=(kt == 0), stop=(kt == KT - 1))
                            return last
                        k.op("tensor", mmu, reads=r_slot[si_u] + [r_xs],
                             writes=[r_ps[pi]])
                        ssl2 = slice(sl2 * 512, (sl2 + 1) * 512)
                        if e < 16:
                            k.op("vector", lambda v, pi=pi, ssl2=ssl2:
                                 v.tensor_tensor(
                                     out=u512[:], in0=ps[pi][:],
                                     in1=comb_bc[:, ssl2], op=OP.mult),
                                 reads=[r_ps[pi], r_cbc], writes=[r_u512])
                            k.op("vector", lambda v, ft=ft, ssl2=ssl2:
                                 v.tensor_tensor(
                                     out=a_bf[:, ft, ssl2], in0=u512[:],
                                     in1=silu_g[:, ft, ssl2], op=OP.mult),
                                 reads=[r_u512, r_silu[ft]],
                                 writes=[r_abf[ft]])
                        else:
                            k.op("vector", lambda v, ft=ft, pi=pi, ssl2=ssl2:
                                 v.tensor_tensor(
                                     out=a_bf[:, ft, ssl2], in0=ps[pi][:],
                                     in1=silu_g[:, ft, ssl2], op=OP.mult),
                                 reads=[r_ps[pi], r_silu[ft]],
                                 writes=[r_abf[ft]])
                dps = [ps[4], ps[5], ps_sq]
                rdps = [r_ps[4], r_ps[5], r_pssq]
                for ht in range(KT):
                    for sl2 in range(CHUNK // 512):
                        pi = (ht * 2 + sl2) % 3
                        pps, rpp = dps[pi], rdps[pi]

                        def mmd(t, ht=ht, sl2=sl2, pps=pps, si_d=si_d):
                            last = None
                            for fk in range(FT):
                                last = t.matmul(
                                    out=pps[:],
                                    lhsT=slot[si_d][:, fk * D + ht * 128:
                                                    fk * D + (ht + 1) * 128],
                                    rhs=a_bf[:, fk, sl2 * 512:(sl2 + 1) * 512],
                                    start=(fk == 0), stop=(fk == FT - 1))
                            return last
                        k.op("tensor", mmd, reads=r_slot[si_d][0:FT] + r_abf,
                             writes=[rpp])
                        ssl2 = slice(sl2 * 512, (sl2 + 1) * 512)
                        if e == 0:
                            k.op("vector", lambda v, ht=ht, pps=pps, ssl2=ssl2:
                                 v.tensor_copy(
                                     out=y_acc[:, ht, ssl2], in_=pps[:]),
                                 reads=[rpp], writes=[r_yacc[ht]])
                        else:
                            k.op("vector", lambda v, ht=ht, pps=pps, ssl2=ssl2:
                                 v.tensor_tensor(
                                     out=y_acc[:, ht, ssl2],
                                     in0=y_acc[:, ht, ssl2], in1=pps[:],
                                     op=OP.add),
                                 reads=[rpp, r_yacc[ht]],
                                 writes=[r_yacc[ht]])
            scE.close()
            if "d_y" in dbg:
                k.op("sync", lambda q, c0_=c0_: q.dma_start(
                    out=dbg["d_y"].rearrange("(kt p) t -> p kt t", p=128)
                    [:, :, c0_:c0_ + CHUNK], in_=y_acc[:]),
                    reads=r_yacc, dma=True)

            scD = st.enter_context(ExitStack())
            y4T = [sb(f"y4T{ch}_{i}", [128, D], F32, scD) for i in range(2)]
            x1r = [sb(f"x1r{ch}_{i}", [128, D], F32, scD) for i in range(2)]
            dumpD = [sb(f"dumpD{ch}_{i}", [128, D], F32, scD) for i in range(2)]
            fD = k.fence()
            r_y4T = [Res(f"y4T{i}", fD) for i in range(2)]
            r_x1r = [Res(f"x1r{i}", fD) for i in range(2)]
            r_dumpD = [Res(f"dumpD{i}", fD) for i in range(2)]
            for tb0 in range(0, CHUNK // 128, 2):
                tts = (ch * (CHUNK // 128) + tb0,
                       ch * (CHUNK // 128) + tb0 + 1)
                for i in range(2):
                    k.op("sync", lambda q, tt=tts[i], xb=x1r[i]: q.dma_start(
                        out=xb[:], in_=x1_scr[tt * 128:(tt + 1) * 128, :]),
                        reads=[r_x1scr[tts[i]]], writes=[r_x1r[i]], dma=True)
                for i in range(2):
                    tb = tb0 + i
                    for qd in range(4):
                        pb2 = 2 + qd % 2

                        def trd(t, qd=qd, tb=tb, pb2=pb2):
                            last = None
                            for j in range(4):
                                ht = qd * 4 + j
                                last = t.transpose(
                                    out=ps[pb2][:, j * 128:(j + 1) * 128],
                                    in_=y_acc[:, ht, tb * 128:(tb + 1) * 128],
                                    identity=idf[:])
                            return last
                        k.op("tensor", trd,
                             reads=[r_yacc[qd * 4 + j] for j in range(4)]
                             + [r_idf], writes=[r_ps[pb2]])
                        k.op("scalar", lambda s, qd=qd, pb2=pb2, yT=y4T[i]:
                             s.copy(out=yT[:, qd * 512:(qd + 1) * 512],
                                    in_=ps[pb2][:]),
                             reads=[r_ps[pb2]], writes=[r_y4T[i]])
                for i in range(2):
                    ln_pre(y4T[i], r_y4T[i], i)
                for i in range(2):
                    ln_post(i)
                for i in range(2):
                    ln_apply(y4T[i][:], r_y4T[i], dumpD[i][:], r_dumpD[i], i)
                for i in range(2):
                    k.op("vector", lambda v, xb=x1r[i], dp=dumpD[i]:
                         v.tensor_tensor(
                             out=dp[:], in0=dp[:], in1=xb[:], op=OP.add),
                         reads=[r_dumpD[i], r_x1r[i]], writes=[r_dumpD[i]])
                    all_out_toks.append(k.op(
                        "sync", lambda q, tt=tts[i], dp=dumpD[i]: q.dma_start(
                            out=out[tt * 128:(tt + 1) * 128, :], in_=dp[:]),
                        reads=[r_dumpD[i]], dma=True))
            scD.close()

        for ch in range(NCH):
            emit_chunk(ch)
        scC.close()

        blk = st.enter_context(nc.Block())
        k.finalize(blk, final_eng="sync", final_tokens=all_out_toks)
    return nc


def _panelize(w):
    return np.ascontiguousarray(
        np.asarray(w).reshape(KT, 128, KT, 128).transpose(2, 1, 0, 3)
        .reshape(D, D))


def _shared_in(inputs):
    gw = np.asarray(inputs["L0_ffn_expert_gate_w"])
    uw = np.asarray(inputs["L0_ffn_expert_up_w"])
    dw = np.asarray(inputs["L0_ffn_expert_down_w"])
    sg = np.asarray(inputs["L0_ffn_shared_gate_w"])
    su = np.asarray(inputs["L0_ffn_shared_up_w"])
    sd = np.asarray(inputs["L0_ffn_shared_down_w"])
    return {
        "wr": _panelize(inputs["L0_attn_Wr"]),
        "wf": _panelize(inputs["L0_attn_Wf"]),
        "wc": _panelize(inputs["L0_attn_Wc"]),
        "wo": _panelize(inputs["L0_attn_Wo"]),
        "lnsc": np.ascontiguousarray(
            np.asarray(inputs["L0_attn_ln_scale"]).reshape(KT, 128).T),
        "rw_in": np.asarray(inputs["L0_ffn_router_w"]),
        "rb_in": np.asarray(inputs["L0_ffn_router_bias"]),
        "wg_in": np.ascontiguousarray(np.concatenate(
            [gw.transpose(0, 2, 1).reshape(16 * D, FF), sg.T], axis=0)),
        "wu_in": np.ascontiguousarray(np.concatenate(
            [uw.transpose(0, 2, 1).reshape(16 * D, FF), su.T], axis=0)),
        "wd_in": np.ascontiguousarray(np.concatenate(
            [dw.transpose(0, 2, 1).reshape(16 * FF, D), sd.T], axis=0)),
        "id_bf": np.eye(128, dtype=ml_dtypes.bfloat16),
        "id_f32": np.eye(128, dtype=np.float32),
        "ones_f32": np.ones((128, 1), dtype=np.float32),
    }


def _in_map(inputs, b, shared=None):
    if shared is None:
        shared = _shared_in(inputs)
    x = np.asarray(inputs["L0_attn_x"])
    return {"x_in": np.ascontiguousarray(x[b]), **shared}


def kernel(**inputs):
    nc = build_nc()
    shared = _shared_in(inputs)
    in_maps = [_in_map(inputs, b, shared) for b in range(NCORES)]
    res = run_bass_kernel_spmd(nc, in_maps, core_ids=list(range(NCORES)))
    return np.stack([r["out"] for r in res.results], axis=0)


# version 14
# speedup vs baseline: 1.1852x; 1.1852x over previous
import numpy as np
import ml_dtypes
from contextlib import ExitStack

import concourse.bass as bass
import concourse.mybir as mybir
from concourse.alu_op_type import AluOpType as OP
from concourse.bass_utils import run_bass_kernel_spmd

F32 = mybir.dt.float32
BF16 = mybir.dt.bfloat16
AF = mybir.ActivationFunctionType
AX = mybir.AxisListType

D = 2048
S = 2048
KT = 16
NE = 17
FF = 1024
FT = 8
CHUNK = 1024
NCH = S // CHUNK
EPS = 1e-5
NCORES = 8

ENGINES = ("tensor", "vector", "scalar", "gpsimd", "sync")


class Res:
    __slots__ = ("name", "writer", "readers")

    def __init__(self, name, fence=()):
        self.name = name
        self.writer = None
        self.readers = list(fence)


class Token:
    __slots__ = ("sem", "val", "vc")

    def __init__(self, sem, val, vc):
        self.sem = sem
        self.val = val
        self.vc = vc


class KBuild:
    N_DMA_SEMS = {"sync": 6, "scalar": 4, "gpsimd": 4}

    def __init__(self, nc, stack):
        self.nc = nc
        self.sems = {}
        for e in ENGINES:
            self.sems["c_" + e] = stack.enter_context(nc.semaphore("c_" + e))
        self.qsems = {}
        self.rot = {}
        for q, n in self.N_DMA_SEMS.items():
            self.qsems[q] = [f"d_{q}{i}" for i in range(n)]
            for s in self.qsems[q]:
                self.sems[s] = stack.enter_context(nc.semaphore(s))
            self.rot[q] = 0
        self.counts = {s: 0 for s in self.sems}
        self.know = {e: {s: 0 for s in self.sems} for e in ENGINES}
        self.thunks = {e: [] for e in ENGINES}
        self.n_ops = 0
        self.n_waits = 0

    def barrier(self):
        targets = [(s, self.counts[s]) for s in self.sems
                   if self.counts[s] > 0 and not s.startswith("d_gpsimd")]
        for e in ENGINES:
            wl = [(self.sems[s], v) for s, v in targets]

            def th2(engine, wl=wl):
                for sh, v in wl:
                    engine.wait_ge(sh, v)
            self.thunks[e].append(th2)
            know = self.know[e]
            for s, v in targets:
                if know[s] < v:
                    know[s] = v

    def barrier_token_tensor(self):
        sem = "c_tensor"
        return Token(sem, self.counts[sem], {sem: self.counts[sem]})

    def fence(self):
        toks = []
        for e in ENGINES:
            sem = "c_" + e
            if self.counts[sem]:
                toks.append(Token(sem, self.counts[sem], {sem: self.counts[sem]}))
        for s in list(self.sems):
            if (s.startswith("d_") and self.counts[s]
                    and not s.startswith("d_gpsimd")):
                toks.append(Token(s, self.counts[s], {s: self.counts[s]}))
        return toks

    def op(self, eng, fn, reads=(), writes=(), dma=False, extra_deps=()):
        toks = []
        for r in reads:
            if r.writer is not None:
                toks.append(r.writer)
        for w in writes:
            if w.writer is not None:
                toks.append(w.writer)
            toks.extend(w.readers)
        toks.extend(extra_deps)
        know = self.know[eng]
        if dma:
            qs = self.qsems[eng]
            my_sem = qs[self.rot[eng]]
            self.rot[eng] = (self.rot[eng] + 1) % len(qs)
            skip = set()
            auto = (Token(my_sem, self.counts[my_sem], {})
                    if self.counts[my_sem] > 0 else None)
        else:
            my_sem = "c_" + eng
            skip = set()
            auto = None
        best = {}
        for t in toks:
            if t is None or t.sem in skip:
                continue
            if know.get(t.sem, 0) >= t.val:
                continue
            if t.sem not in best or best[t.sem][0] < t.val:
                best[t.sem] = (t.val, t.vc)
        if auto is not None and know.get(auto.sem, 0) < auto.val:
            if auto.sem not in best or best[auto.sem][0] < auto.val:
                best[auto.sem] = (auto.val, auto.vc)
        for sem, (val, vc) in best.items():
            for s2, v2 in vc.items():
                if know[s2] < v2:
                    know[s2] = v2
            if know[sem] < val:
                know[sem] = val
        self.counts[my_sem] += 16 if dma else 1
        newval = self.counts[my_sem]
        vc = dict(know)
        vc[my_sem] = newval
        tok = Token(my_sem, newval, vc)

        wait_list = [(self.sems[s], v) for s, (v, _) in best.items()]
        inc_handle = self.sems[my_sem]
        inc_amt = 16 if dma else 1

        def thunk(engine, wl=wait_list, fn=fn, ih=inc_handle, ia=inc_amt):
            for sh, v in wl:
                engine.wait_ge(sh, v)
            last = fn(engine)
            assert last is not None, "op fn must return last instruction"
            last.then_inc(ih, ia)

        self.thunks[eng].append(thunk)
        self.n_ops += 1
        self.n_waits += len(wait_list)
        for r in reads:
            r.readers.append(tok)
        for w in writes:
            w.writer = tok
            w.readers = []
        return tok

    def finalize(self, block, final_eng="sync", final_tokens=()):
        if final_tokens:
            know = self.know[final_eng]
            best = {}
            for t in final_tokens:
                if t is None or know.get(t.sem, 0) >= t.val:
                    continue
                if t.sem not in best or best[t.sem] < t.val:
                    best[t.sem] = t.val
            wl = [(self.sems[s], v) for s, v in best.items()]

            def fthunk(engine, wl=wl):
                for sh, v in wl:
                    engine.wait_ge(sh, v)
            self.thunks[final_eng].append(fthunk)

        for q, sems in self.qsems.items():
            pend = [(self.sems[s], self.counts[s]) for s in sems
                    if self.counts[s] > 0]
            if pend:
                def qthunk(engine, pend=pend):
                    for sh, v in pend:
                        engine.wait_ge(sh, v)
                self.thunks[q].append(qthunk)

        for eng in ENGINES:
            ths = self.thunks[eng]
            if not ths:
                continue
            dec = getattr(block, eng)

            @dec
            def _(engine, _ths=ths):
                for th in _ths:
                    th(engine)


def build_nc(debug=()):
    nc = bass.Bass("TRN2", target_bir_lowering=False)

    x_in = nc.dram_tensor("x_in", [S, D], F32, kind="ExternalInput").ap()
    wr = nc.dram_tensor("wr", [D, D], F32, kind="ExternalInput").ap()
    wf = nc.dram_tensor("wf", [D, D], F32, kind="ExternalInput").ap()
    wc = nc.dram_tensor("wc", [D, D], F32, kind="ExternalInput").ap()
    wo = nc.dram_tensor("wo", [D, D], F32, kind="ExternalInput").ap()
    lnsc = nc.dram_tensor("lnsc", [128, KT], F32, kind="ExternalInput").ap()
    rw_in = nc.dram_tensor("rw_in", [16, D], F32, kind="ExternalInput").ap()
    rb_in = nc.dram_tensor("rb_in", [16], F32, kind="ExternalInput").ap()
    wg_in = nc.dram_tensor("wg_in", [NE * D, FF], F32, kind="ExternalInput").ap()
    wu_in = nc.dram_tensor("wu_in", [NE * D, FF], F32, kind="ExternalInput").ap()
    wd_in = nc.dram_tensor("wd_in", [NE * FF, D], F32, kind="ExternalInput").ap()
    id_bf = nc.dram_tensor("id_bf", [128, 128], BF16, kind="ExternalInput").ap()
    id_f32 = nc.dram_tensor("id_f32", [128, 128], F32, kind="ExternalInput").ap()
    ones_f32 = nc.dram_tensor("ones_f32", [128, 1], F32, kind="ExternalInput").ap()
    out = nc.dram_tensor("out", [S, D], F32, kind="ExternalOutput").ap()

    dbg = {}
    for name, shape, dt in [
        ("d_xn1T", [D, S], BF16), ("d_h", [D, S], F32), ("d_p", [D, S], BF16),
        ("d_x1", [S, D], F32), ("d_xn2", [D, S], BF16),
        ("d_comb", [16, S], F32), ("d_y", [D, S], F32),
    ]:
        if name in debug:
            dbg[name] = nc.dram_tensor(name, shape, dt, kind="ExternalOutput").ap()

    wat_scr = {w: nc.dram_tensor(f"w{w}_scr", [D, D], BF16, kind="Internal").ap()
               for w in "fcro"}
    wg_scr = nc.dram_tensor("wg_scr", [NE * D, FF], BF16, kind="Internal").ap()
    wu_scr = nc.dram_tensor("wu_scr", [NE * D, FF], BF16, kind="Internal").ap()
    wd_scr = nc.dram_tensor("wd_scr", [NE * FF, D], BF16, kind="Internal").ap()
    x1_scr = nc.dram_tensor("x1_scr", [S, D], F32, kind="Internal").ap()
    xn2_scr = nc.dram_tensor("xn2_scr", [D, S], BF16, kind="Internal").ap()
    row_scr = nc.dram_tensor("row_scr", [1, S], F32, kind="Internal").ap()
    comb_scr = nc.dram_tensor("comb_scr", [16, S], F32, kind="Internal").ap()

    xn2_scr_r = xn2_scr.rearrange("(kt p) t -> p kt t", p=128)

    with ExitStack() as st:
        k = KBuild(nc, st)

        def sb(name, shape, dt, scope=st, side=None):
            if side is None:
                return scope.enter_context(nc.sbuf_tensor(name, shape, dt))
            return scope.enter_context(
                nc.sbuf_tensor(name, shape, dt, side=side))

        idb = sb("idb", [128, 128], BF16)
        idf = sb("idf", [128, 128], F32)
        ones = sb("ones", [128, 1], F32)
        lnsc_sb = sb("lnsc_sb", [128, KT], F32)
        rb_sb = sb("rb_sb", [16, 1], F32)
        STATN = ("sd", "rstd", "nbias")
        stats2 = [{n: sb(f"st{i}_" + n, [128, 1], F32) for n in STATN}
                  for i in range(2)]
        r_stats2 = [{n: Res(f"st{i}_" + n) for n in STATN} for i in range(2)]
        bnb2 = [sb(f"bnb{i}", [128, 4, 6], F32) for i in range(2)]
        ag2 = [sb(f"ag{i}", [128, 2], F32) for i in range(2)]
        r_bnb2 = [Res(f"bnb{i}") for i in range(2)]
        r_ag2 = [Res(f"ag{i}") for i in range(2)]

        ps = [st.enter_context(nc.psum_tensor(f"ps{i}", [128, 512], F32))
              for i in range(6)]
        ps_sq = st.enter_context(nc.psum_tensor("ps_sq", [128, 512], F32))
        ps_bf = st.enter_context(nc.psum_tensor("ps_bf", [128, 1024], BF16))
        r_ps = [Res(f"ps{i}") for i in range(6)]
        r_pssq = Res("ps_sq")
        r_psbf = Res("ps_bf")

        r_idb, r_idf, r_ones = Res("idb"), Res("idf"), Res("ones")
        r_lnsc, r_rb = Res("lnsc"), Res("rb")

        k.op("sync", lambda q: q.dma_start(out=idb[:], in_=id_bf),
             writes=[r_idb], dma=True)
        k.op("sync", lambda q: q.dma_start(out=idf[:], in_=id_f32),
             writes=[r_idf], dma=True)
        k.op("sync", lambda q: q.dma_start(out=ones[:], in_=ones_f32),
             writes=[r_ones], dma=True)
        k.op("sync", lambda q: q.dma_start(out=lnsc_sb[:], in_=lnsc),
             writes=[r_lnsc], dma=True)
        k.op("sync", lambda q: q.dma_start(
            out=rb_sb[:], in_=rb_in.rearrange("(e o) -> e o", o=1)),
            writes=[r_rb], dma=True)

        def d2d(dst, srcap, rows, piece, deps=()):
            toks = []
            for a in range(0, rows, piece):
                b = min(a + piece, rows)
                toks.append(k.op(
                    "gpsimd", lambda g, dst=dst, srcap=srcap, a=a, b=b:
                    g.dma_start(out=dst[a:b, :], in_=srcap[a:b, :]),
                    dma=True, extra_deps=deps))
            return toks

        wmap_in = {"f": wf, "c": wc, "r": wr, "o": wo}
        t_wat = {w: d2d(wat_scr[w], wmap_in[w], D, 256) for w in "fcr"}
        t_wg = [None] * NE
        t_wu = [None] * NE
        t_wd = [None] * NE

        def emit_expert_d2d(e, deps):
            t_wg[e] = d2d(wg_scr[e * D:(e + 1) * D, :],
                          wg_in[e * D:(e + 1) * D, :], D, 512, deps)
            t_wu[e] = d2d(wu_scr[e * D:(e + 1) * D, :],
                          wu_in[e * D:(e + 1) * D, :], D, 512, deps)
            t_wd[e] = d2d(wd_scr[e * FF:(e + 1) * FF, :],
                          wd_in[e * FF:(e + 1) * FF, :], FF, 512, deps)

        def ln_pre(xt, r_xt, si, vscale=None, r_vscale=None):
            bnb, ag = bnb2[si], ag2[si]
            r_bnb, r_ag = r_bnb2[si], r_ag2[si]
            stats, r_stats = stats2[si], r_stats2[si]
            for j in range(4):
                k.op("vector", lambda v, j=j, bnb=bnb, xt=xt: v.bn_stats(
                    out=bnb[:, j, :], in_=xt[:, j * 512:(j + 1) * 512]),
                    reads=[r_xt], writes=[r_bnb])
            k.op("vector", lambda v, ag=ag, bnb=bnb: v.bn_aggr(
                out=ag[:], in_=bnb[:]), reads=[r_bnb], writes=[r_ag])
            if vscale is not None:
                k.op("vector", lambda v, ag=ag, vscale=vscale: v.tensor_scalar(
                    out=ag[:, 1:2], in0=ag[:, 1:2], scalar1=vscale,
                    scalar2=EPS, op0=OP.mult, op1=OP.add),
                    reads=[r_ag, r_vscale], writes=[r_ag])
            else:
                k.op("vector", lambda v, ag=ag: v.tensor_scalar(
                    out=ag[:, 1:2], in0=ag[:, 1:2], scalar1=EPS, scalar2=None,
                    op0=OP.add), reads=[r_ag], writes=[r_ag])
            k.op("scalar", lambda s, ag=ag, sd=stats["sd"]: s.activation(
                out=sd[:, 0:1], in_=ag[:, 1:2], func=AF.Sqrt),
                reads=[r_ag], writes=[r_stats["sd"]])

        def ln_post(si, mscale=None, r_mscale=None):
            stats, r_stats = stats2[si], r_stats2[si]
            ag, r_ag = ag2[si], r_ag2[si]
            k.op("vector", lambda v, stats=stats: v.reciprocal(
                stats["rstd"][:, 0:1], stats["sd"][:, 0:1]),
                reads=[r_stats["sd"]], writes=[r_stats["rstd"]])
            if mscale is not None:
                k.op("vector", lambda v, stats=stats, mscale=mscale:
                     v.tensor_tensor(
                         out=stats["rstd"][:, 0:1], in0=stats["rstd"][:, 0:1],
                         in1=mscale, op=OP.mult),
                     reads=[r_stats["rstd"], r_mscale],
                     writes=[r_stats["rstd"]])
            k.op("vector", lambda v, stats=stats, ag=ag: v.tensor_scalar(
                out=stats["nbias"][:, 0:1], in0=ag[:, 0:1],
                scalar1=stats["rstd"][:, 0:1], scalar2=-1.0,
                op0=OP.mult, op1=OP.mult),
                reads=[r_ag, r_stats["rstd"]], writes=[r_stats["nbias"]])

        def ln_stats(xt, r_xt, dump, r_dump, si=0):
            ln_pre(xt, r_xt, si)
            ln_post(si)

        def ln_apply(src, r_src, dst, r_dst, si=0):
            stats = stats2[si]
            r_stats = r_stats2[si]
            k.op("vector", lambda v: v.tensor_scalar(
                out=dst, in0=src, scalar1=stats["rstd"][:, 0:1],
                scalar2=stats["nbias"][:, 0:1], op0=OP.mult, op1=OP.add),
                reads=[r_src, r_stats["rstd"], r_stats["nbias"]],
                writes=[r_dst])

        fenceA = k.fence()
        scA = st.enter_context(ExitStack())
        p_bf = sb("p_bf", [128, KT, S], BF16, scA)
        scAx = st.enter_context(ExitStack())
        x_n1T = sb("x_n1T", [128, KT, S], BF16, scAx)
        r_xn1T = [Res(f"x_n1T{t}", fenceA) for t in range(16)]
        r_pbf = [Res(f"p{m}", fenceA) for m in range(KT)]
        r_dump = Res("dump", fenceA)

        scA1 = st.enter_context(ExitStack())
        xtile = [sb(f"xtile{i}", [128, D], F32, scA1) for i in range(4)]
        xn1t = [sb(f"xn1t{i}", [128, D], BF16, scA1) for i in range(2)]
        r_xtile = [Res(f"xtile{i}", fenceA) for i in range(4)]
        r_xn1t = [Res(f"xn1t{i}", fenceA) for i in range(2)]

        def a1_load(tt):
            b4 = tt % 4
            k.op("sync", lambda q, tt=tt, xb=xtile[b4]: q.dma_start(
                out=xb[:], in_=x_in[tt * 128:(tt + 1) * 128, :]),
                writes=[r_xtile[b4]], dma=True)

        def a1_tiles(tp):
            for i in range(2):
                if tp + 2 + i < 16:
                    a1_load(tp + 2 + i)
            for i in range(2):
                b4 = (tp + i) % 4
                ln_pre(xtile[b4], r_xtile[b4], i)
            for i in range(2):
                ln_post(i)
            for i in range(2):
                b4 = (tp + i) % 4
                ln_apply(xtile[b4][:], r_xtile[b4], xn1t[i][:], r_xn1t[i], i)
            for i in range(2):
                tt = tp + i
                for half in range(2):
                    def tr(t, tt=tt, half=half, x1t=xn1t[i]):
                        last = None
                        for j in range(8):
                            kt = half * 8 + j
                            last = t.transpose(
                                out=ps_bf[:, j * 128:(j + 1) * 128],
                                in_=x1t[:, kt * 128:(kt + 1) * 128],
                                identity=idb[:])
                        return last
                    k.op("tensor", tr, reads=[r_xn1t[i], r_idb],
                         writes=[r_psbf])
                    k.op("scalar", lambda s, tt=tt, half=half: s.copy(
                        out=x_n1T[:, half * 8:(half + 1) * 8,
                                  tt * 128:(tt + 1) * 128],
                        in_=ps_bf.ap().rearrange("p (b c) -> p b c", c=128)),
                        reads=[r_psbf], writes=[r_xn1T[tt]])

        a1_load(0)
        a1_load(1)
        for tp in range(0, 16, 2):
            a1_tiles(tp)

        if "d_xn1T" in dbg:
            k.op("sync", lambda q: q.dma_start(
                out=dbg["d_xn1T"].rearrange("(kt p) t -> p kt t", p=128),
                in_=x_n1T[:]), reads=r_xn1T, dma=True)

        scA1.close()
        scSQ = st.enter_context(ExitStack())
        Hbuf = sb("Hbuf", [128, S], F32, scSQ, side="right")
        sqacc = sb("sqacc", [128, 512], F32, scSQ, side="right")
        sqacc3 = sb("sqacc3", [1, 512], F32, scSQ, side="right")
        scA2 = st.enter_context(ExitStack())
        f0b = [sb(f"f0_{i}", [128, S], F32, scA2) for i in range(2)]
        c0b = [sb(f"c0_{i}", [128, S], F32, scA2) for i in range(2)]
        lvf = sb("lvf", [128, S], F32, scA2)
        lvc = sb("lvc", [128, S], F32, scA2)
        tmpA = sb("tmpA", [128, 512], F32, scA2)
        pan = {w: [sb(f"pan_{w}{i}", [128, KT * 128], BF16, scA2)
                   for i in range(2 if w == "r" else 1)] for w in "fcr"}
        r_f0b = [Res(f"f0_{i}", fenceA) for i in range(2)]
        r_c0b = [Res(f"c0_{i}", fenceA) for i in range(2)]
        r_lvf = Res("lvf", fenceA)
        r_lvc = Res("lvc", fenceA)
        r_H = Res("H", fenceA)
        r_tmpA = Res("tmpA", fenceA)
        r_sqacc = Res("sqacc", fenceA)
        r_sqacc3 = Res("sqacc3", fenceA)
        r_pan = {w: [Res(f"pan_{w}{i}", fenceA)
                     for i in range(2 if w == "r" else 1)] for w in "fcr"}



        def emit_scan(M, f0, c0, r_f0, r_c0):
            offs = {}
            off = 0
            n = S
            lvl = 0
            prev_f, prev_c = f0, c0
            prev_rf, prev_rc = r_f0, r_c0
            while n >= 2:
                n2 = n // 2
                lvl += 1
                offs[lvl] = (off, n2)
                cf = lvf[:, off:off + n2]
                cc = lvc[:, off:off + n2]
                pf, pc = prev_f, prev_c
                if lvl == 1:
                    af = pf[:, 0:S:2]; bf_ = pf[:, 1:S:2]
                    ac = pc[:, 0:S:2]; bc_ = pc[:, 1:S:2]
                else:
                    po, pn = offs[lvl - 1]
                    af = lvf[:, po:po + pn:2]; bf_ = lvf[:, po + 1:po + pn:2]
                    ac = lvc[:, po:po + pn:2]; bc_ = lvc[:, po + 1:po + pn:2]
                for q0 in range(0, n2, 512):
                    q1 = min(q0 + 512, n2)
                    w_ = q1 - q0
                    k.op("vector", lambda v, ac=ac, bc_=bc_, q0=q0, q1=q1, w_=w_:
                         v.tensor_tensor(
                             out=tmpA[:, 0:w_], in0=ac[:, q0:q1],
                             in1=bc_[:, q0:q1], op=OP.subtract),
                         reads=[prev_rf, prev_rc], writes=[r_tmpA])
                    k.op("vector", lambda v, bf_=bf_, q0=q0, q1=q1, w_=w_:
                         v.tensor_tensor(
                             out=tmpA[:, 0:w_], in0=tmpA[:, 0:w_],
                             in1=bf_[:, q0:q1], op=OP.mult),
                         reads=[r_tmpA, prev_rf], writes=[r_tmpA])
                    k.op("vector", lambda v, cc=cc, bc_=bc_, q0=q0, q1=q1, w_=w_:
                         v.tensor_tensor(
                             out=cc[:, q0:q1], in0=tmpA[:, 0:w_],
                             in1=bc_[:, q0:q1], op=OP.add),
                         reads=[r_tmpA, prev_rc], writes=[r_lvc])
                k.op("vector", lambda v, cf=cf, af=af, bf_=bf_: v.tensor_tensor(
                    out=cf, in0=af, in1=bf_, op=OP.mult),
                    reads=[prev_rf], writes=[r_lvf])
                prev_f = prev_c = None
                prev_rf, prev_rc = r_lvf, r_lvc
                off += n2
                n = n2
            for lvl in range(11, -1, -1):
                step = 1 << lvl
                if lvl == 0:
                    e_f, e_c, nl = f0, c0, S
                    r_ef, r_ec = r_f0, r_c0
                else:
                    o, nl = offs[lvl]
                    e_f = lvf[:, o:o + nl]
                    e_c = lvc[:, o:o + nl]
                    r_ef, r_ec = r_lvf, r_lvc
                k.op("vector", lambda v, e_c=e_c, step=step: v.tensor_copy(
                    out=Hbuf[:, step - 1:step], in_=e_c[:, 0:1]),
                    reads=[r_ec], writes=[r_H])
                ncomb = nl // 2 - 1
                if ncomb <= 0:
                    continue
                cs = Hbuf[:, 2 * step - 1: 2 * step - 1 + (ncomb - 1) * 2 * step + 1: 2 * step]
                fe = e_f[:, 2:2 + (ncomb - 1) * 2 + 1:2]
                ce = e_c[:, 2:2 + (ncomb - 1) * 2 + 1:2]
                hw_ = Hbuf[:, 3 * step - 1: 3 * step - 1 + (ncomb - 1) * 2 * step + 1: 2 * step]
                for q0 in range(0, ncomb, 512):
                    q1 = min(q0 + 512, ncomb)
                    w_ = q1 - q0
                    k.op("vector", lambda v, cs=cs, ce=ce, q0=q0, q1=q1, w_=w_:
                         v.tensor_tensor(
                             out=tmpA[:, 0:w_], in0=cs[:, q0:q1],
                             in1=ce[:, q0:q1], op=OP.subtract),
                         reads=[r_H, r_ec], writes=[r_tmpA])
                    k.op("vector", lambda v, fe=fe, q0=q0, q1=q1, w_=w_:
                         v.tensor_tensor(
                             out=tmpA[:, 0:w_], in0=tmpA[:, 0:w_],
                             in1=fe[:, q0:q1], op=OP.mult),
                         reads=[r_tmpA, r_ef], writes=[r_tmpA])
                    k.op("vector", lambda v, hw_=hw_, ce=ce, q0=q0, q1=q1, w_=w_:
                         v.tensor_tensor(
                             out=hw_[:, q0:q1], in0=tmpA[:, 0:w_],
                             in1=ce[:, q0:q1], op=OP.add),
                         reads=[r_tmpA, r_ec], writes=[r_H])

        for M in range(KT):
            f0, c0 = f0b[M % 2], c0b[M % 2]
            r_f0, r_c0 = r_f0b[M % 2], r_c0b[M % 2]
            for wi, wn in enumerate("fcr"):
                mpw = M % len(pan[wn])
                k.op("sync", lambda q, wn=wn, M=M, mpw=mpw: q.dma_start(
                    out=pan[wn][mpw][:],
                    in_=wat_scr[wn][M * 128:(M + 1) * 128, :]),
                    extra_deps=[t_wat[wn][M // 2]], writes=[r_pan[wn][mpw]],
                    dma=True)
            for g in range(4):
                gsl = slice(g * 512, (g + 1) * 512)
                xr = [r_xn1T[g * 4 + j] for j in range(4)]
                pF = ps[g % 2]; rF = r_ps[g % 2]
                pC = ps[2 + g % 2]; rC = r_ps[2 + g % 2]
                pR = ps[4 + g % 2]; rR = r_ps[4 + g % 2]
                for wn, pp, rp in (("f", pF, rF), ("c", pC, rC), ("r", pR, rR)):
                    mpw = M % len(pan[wn])

                    def mm(t, wn=wn, pp=pp, g=g, mpw=mpw):
                        last = None
                        for kt in range(KT):
                            last = t.matmul(
                                out=pp[:],
                                lhsT=pan[wn][mpw][:, kt * 128:(kt + 1) * 128],
                                rhs=x_n1T[:, kt, g * 512:(g + 1) * 512],
                                start=(kt == 0), stop=(kt == KT - 1))
                        return last
                    mmt = k.op("tensor", mm, reads=[r_pan[wn][mpw]] + xr,
                               writes=[rp])
                if g == 0 and M == 0:
                    t_wat["o"] = d2d(wat_scr["o"], wmap_in["o"], D, 256,
                                     deps=[mmt])
                if g == 0 and M < NE:
                    emit_expert_d2d(M, [mmt])
                k.op("scalar", lambda s, pF=pF, f0=f0, gsl=gsl: s.activation(
                    out=f0[:, gsl], in_=pF[:], func=AF.Tanh, scale=0.5),
                    reads=[rF], writes=[r_f0])
                k.op("vector", lambda v, f0=f0, gsl=gsl: v.tensor_scalar(
                    out=f0[:, gsl], in0=f0[:, gsl], scalar1=0.5, scalar2=0.5,
                    op0=OP.mult, op1=OP.add),
                    reads=[r_f0], writes=[r_f0])
                k.op("scalar", lambda s, pC=pC, c0=c0, gsl=gsl: s.activation(
                    out=c0[:, gsl], in_=pC[:], func=AF.Silu),
                    reads=[rC], writes=[r_c0])
                k.op("scalar", lambda s, pR=pR, M=M, gsl=gsl: s.activation(
                    out=p_bf[:, M, gsl], in_=pR[:], func=AF.Silu),
                    reads=[rR], writes=[r_pbf[M]])
            def emit_sq_square(Mprev):
                k.op("scalar", lambda s: s.activation(
                    out=Hbuf[:], in_=Hbuf[:], func=AF.Square),
                    reads=[r_H], writes=[r_H])

            def emit_sq_slab(Mprev, j):
                dst = (sqacc3[0:1, :] if j == 3
                       else sqacc[j * 32:j * 32 + 1, :])
                rdst = r_sqacc3 if j == 3 else r_sqacc
                k.op("tensor", lambda t, j=j: t.matmul(
                    out=ps_sq[0:1, :], lhsT=ones[:],
                    rhs=Hbuf[:, j * 512:(j + 1) * 512],
                    start=True, stop=True),
                    reads=[r_H, r_ones], writes=[r_pssq])
                if Mprev == 0:
                    k.op("vector", lambda v, dst=dst: v.tensor_copy(
                        out=dst, in_=ps_sq[0:1, :]),
                        reads=[r_pssq], writes=[rdst])
                else:
                    k.op("vector", lambda v, dst=dst: v.tensor_tensor(
                        out=dst, in0=dst, in1=ps_sq[0:1, :], op=OP.add),
                        reads=[r_pssq, rdst], writes=[rdst])

            def emit_sq(Mprev):
                emit_sq_square(Mprev)
                for j in range(4):
                    emit_sq_slab(Mprev, j)

            if M > 0:
                emit_sq(M - 1)
            emit_scan(M, f0, c0, r_f0, r_c0)
            if "d_h" in dbg:
                k.op("sync", lambda q, M=M: q.dma_start(
                    out=dbg["d_h"][M * 128:(M + 1) * 128, :], in_=Hbuf[:]),
                    reads=[r_H], dma=True)
            k.op("vector", lambda v, M=M: v.scalar_tensor_tensor(
                out=p_bf[:, M, :], in0=Hbuf[:], scalar=lnsc_sb[:, M:M + 1],
                in1=p_bf[:, M, :], op0=OP.mult, op1=OP.mult),
                reads=[r_H, r_lnsc, r_pbf[M]], writes=[r_pbf[M]])

        if "d_p" in dbg:
            k.op("sync", lambda q: q.dma_start(
                out=dbg["d_p"].rearrange("(kt p) t -> p kt t", p=128),
                in_=p_bf[:]), reads=r_pbf, dma=True)

        final_sq = [lambda: (emit_sq(KT - 1),
                             emit_expert_d2d(16, [k.barrier_token_tensor()]))]

        r_rowscr = Res("row_scr")

        def emit_row_stores():
            for j in range(4):
                srcx = (sqacc3[0:1, :] if j == 3
                        else sqacc[j * 32:j * 32 + 1, :])
                k.op("sync", lambda q, j=j, srcx=srcx: q.dma_start(
                    out=row_scr[0:1, j * 512:(j + 1) * 512], in_=srcx),
                    reads=[r_sqacc3 if j == 3 else r_sqacc],
                    writes=[r_rowscr], dma=True)

        scA2.close()
        scAx.close()

        fence3 = k.fence()
        scA3 = st.enter_context(ExitStack())
        y2s = sb("y2s", [128, KT, 512], F32, scA3)
        craw = [sb(f"craw{i}", [128, 1], F32, scA3) for i in range(2)]
        csqb = [sb(f"csq{i}", [128, 1], F32, scA3) for i in range(2)]
        ccol = [sb(f"ccol{i}", [128, 1], F32, scA3) for i in range(2)]
        r_craw = [Res(f"craw{i}", fence3) for i in range(2)]
        r_csq = [Res(f"csq{i}", fence3) for i in range(2)]
        r_ccol = [Res(f"ccol{i}", fence3) for i in range(2)]
        y2T = [sb(f"y2T{i}", [128, D], F32, scA3) for i in range(3)]
        xrt = [sb(f"xrt{i}", [128, D], F32, scA3) for i in range(2)]
        x1b = [sb(f"x1b{i}", [128, D], F32, scA3) for i in range(2)]
        xn2b = [sb(f"xn2b{i}", [128, D], BF16, scA3) for i in range(2)]
        xn2st = [sb(f"xn2st{i}", [128, KT, 128], BF16, scA3) for i in range(2)]
        dump3 = [sb(f"dump3{i}", [128, D], F32, scA3) for i in range(2)]
        pano = [sb(f"pano{i}", [128, KT * 128], BF16, scA3) for i in range(2)]
        r_y2s = [Res(f"y2s{m}", fence3) for m in range(KT)]
        r_y2T = [Res(f"y2T{i}", fence3) for i in range(3)]
        r_xrt = [Res(f"xrt{i}", fence3) for i in range(2)]
        r_x1b = [Res(f"x1b{i}", fence3) for i in range(2)]
        r_xn2b = [Res(f"xn2b{i}", fence3) for i in range(2)]
        r_xn2st = [Res(f"xn2st{i}", fence3) for i in range(2)]
        r_dump3 = [Res(f"dump3{i}", fence3) for i in range(2)]
        r_pano = [Res(f"pano{i}", fence3) for i in range(2)]
        r_x1scr = [Res(f"x1_scr{t}") for t in range(16)]
        r_xn2scr = [Res(f"xn2_scr{t}") for t in range(16)]

        def c_chain(tt, i):
            k.op("sync", lambda q, tt=tt, i=i: q.dma_start(
                out=craw[i][:, 0:1],
                in_=row_scr[0:1, tt * 128:(tt + 1) * 128]
                .rearrange("a (p o) -> (a p) o", o=1)),
                reads=[r_rowscr], writes=[r_craw[i]], dma=True)
            k.op("vector", lambda v, i=i: v.tensor_scalar(
                out=csqb[i][:, 0:1], in0=craw[i][:, 0:1], scalar1=1.0 / D,
                scalar2=EPS, op0=OP.mult, op1=OP.add),
                reads=[r_craw[i]], writes=[r_csq[i]])
            k.op("vector", lambda v, i=i: v.reciprocal(
                csqb[i][:, 0:1], csqb[i][:, 0:1]),
                reads=[r_csq[i]], writes=[r_csq[i]])
            k.op("scalar", lambda s, i=i: s.activation(
                out=ccol[i][:, 0:1], in_=csqb[i][:, 0:1], func=AF.Sqrt),
                reads=[r_csq[i]], writes=[r_ccol[i]])

        for sl in range(4):
            ssl = slice(sl * 512, (sl + 1) * 512)
            for M in range(KT):
                pb = M % 2
                k.op("sync", lambda q, M=M: q.dma_start(
                    out=pano[M % 2][:],
                    in_=wat_scr["o"][M * 128:(M + 1) * 128, :]),
                    extra_deps=[t_wat["o"][M // 2]], writes=[r_pano[M % 2]],
                    dma=True)
                pp, rp = ps[pb], r_ps[pb]
                lastw = r_pbf[KT - 1].writer
                lw = (k.sems[lastw.sem], lastw.val) if lastw is not None else None

                def mmo(t, M=M, pb=pb, sl=sl, lw=lw):
                    last = None
                    for kt in range(KT):
                        if kt == KT - 1 and lw is not None:
                            t.wait_ge(lw[0], lw[1])
                        last = t.matmul(
                            out=ps[pb][:],
                            lhsT=pano[M % 2][:, kt * 128:(kt + 1) * 128],
                            rhs=p_bf[:, kt, sl * 512:(sl + 1) * 512],
                            start=(kt == 0), stop=(kt == KT - 1))
                    return last
                tok_mmo = k.op("tensor", mmo,
                               reads=[r_pano[M % 2]] + r_pbf[0:KT - 1],
                               writes=[rp])
                r_pbf[KT - 1].readers.append(tok_mmo)
                if sl == 0 and M == 5 and final_sq[0] is not None:
                    final_sq[0]()
                    emit_row_stores()
                    final_sq[0] = None
                k.op("scalar", lambda s, M=M, pb=pb: s.copy(
                    out=y2s[:, M, :], in_=ps[pb][:]),
                    reads=[rp], writes=[r_y2s[M]])
            for tb0 in range(0, 4, 2):
                tts = (sl * 4 + tb0, sl * 4 + tb0 + 1)
                for i in range(2):
                    k.op("sync", lambda q, tt=tts[i], xb=xrt[i]: q.dma_start(
                        out=xb[:], in_=x_in[tt * 128:(tt + 1) * 128, :]),
                        writes=[r_xrt[i]], dma=True)
                    c_chain(tts[i], i)
                for i in range(2):
                    tb = tb0 + i
                    y4i = tb % 3
                    for qd in range(4):
                        pb2 = 2 + qd % 2

                        def trq(t, qd=qd, tb=tb, pb2=pb2):
                            last = None
                            for j in range(4):
                                M = qd * 4 + j
                                last = t.transpose(
                                    out=ps[pb2][:, j * 128:(j + 1) * 128],
                                    in_=y2s[:, M, tb * 128:(tb + 1) * 128],
                                    identity=idf[:])
                            return last
                        k.op("tensor", trq,
                             reads=[r_y2s[qd * 4 + j] for j in range(4)]
                             + [r_idf], writes=[r_ps[pb2]])
                        k.op("scalar", lambda s, qd=qd, pb2=pb2, yT=y2T[y4i]:
                             s.copy(out=yT[:, qd * 512:(qd + 1) * 512],
                                    in_=ps[pb2][:]),
                             reads=[r_ps[pb2]], writes=[r_y2T[y4i]])
                for i in range(2):
                    y4i = (tb0 + i) % 3
                    ln_pre(y2T[y4i], r_y2T[y4i], i,
                           vscale=csqb[i][:, 0:1], r_vscale=r_csq[i])
                for i in range(2):
                    ln_post(i, mscale=ccol[i][:, 0:1], r_mscale=r_ccol[i])
                for i in range(2):
                    y4i = (tb0 + i) % 3
                    ln_apply(y2T[y4i][:], r_y2T[y4i], dump3[i][:],
                             r_dump3[i], i)
                for i in range(2):
                    k.op("vector", lambda v, xb=xrt[i], dp=dump3[i],
                         xob=x1b[i]: v.tensor_tensor(
                             out=xob[:], in0=dp[:], in1=xb[:], op=OP.add),
                         reads=[r_dump3[i], r_xrt[i]], writes=[r_x1b[i]])
                    k.op("sync", lambda q, tt=tts[i], xob=x1b[i]: q.dma_start(
                        out=x1_scr[tt * 128:(tt + 1) * 128, :], in_=xob[:]),
                        reads=[r_x1b[i]], writes=[r_x1scr[tts[i]]], dma=True)
                for i in range(2):
                    ln_pre(x1b[i], r_x1b[i], i)
                for i in range(2):
                    ln_post(i)
                for i in range(2):
                    ln_apply(x1b[i][:], r_x1b[i], xn2b[i][:], r_xn2b[i], i)
                for i in range(2):
                    for half in range(2):
                        def trx(t, half=half, nb=xn2b[i]):
                            last = None
                            for j in range(8):
                                kt = half * 8 + j
                                last = t.transpose(
                                    out=ps_bf[:, j * 128:(j + 1) * 128],
                                    in_=nb[:, kt * 128:(kt + 1) * 128],
                                    identity=idb[:])
                            return last
                        k.op("tensor", trx, reads=[r_xn2b[i], r_idb],
                             writes=[r_psbf])
                        k.op("scalar", lambda s, half=half, nst=xn2st[i]:
                             s.copy(out=nst[:, half * 8:(half + 1) * 8, :],
                                    in_=ps_bf.ap()
                                    .rearrange("p (b c) -> p b c", c=128)),
                             reads=[r_psbf], writes=[r_xn2st[i]])
                    k.op("sync", lambda q, tt=tts[i], nst=xn2st[i]:
                         q.dma_start(
                             out=xn2_scr_r[:, :, tt * 128:(tt + 1) * 128],
                             in_=nst[:]),
                         reads=[r_xn2st[i]], writes=[r_xn2scr[tts[i]]],
                         dma=True)

        if "d_x1" in dbg:
            k.op("sync", lambda q: q.dma_start(out=dbg["d_x1"], in_=x1_scr),
                 reads=r_x1scr, dma=True)
        if "d_xn2" in dbg:
            k.op("sync", lambda q: q.dma_start(out=dbg["d_xn2"], in_=xn2_scr),
                 reads=r_xn2scr, dma=True)

        scA3.close()
        scSQ.close()
        scA.close()

        fenceR = k.fence()
        scR = st.enter_context(ExitStack())
        xsl = sb("xsl", [128, KT, 512], BF16, scR)
        rwp = sb("rwp", [128, D], BF16, scR)
        rT = sb("rT", [128, KT * 16], BF16, scR)
        sco = sb("sco", [128, S], F32, scR)
        cho = sb("cho", [128, S], F32, scR)
        scot = sb("scot", [128, 16], F32, scR)
        chot = sb("chot", [128, 16], F32, scR)
        mx8 = sb("mx8", [128, 8], F32, scR)
        msk = sb("msk", [128, 16], F32, scR)
        cmb = sb("cmb", [128, 16], F32, scR)
        den = sb("den", [128, 1], F32, scR)
        ccm = sb("ccm", [16, S], F32, scR)
        rwstg = sb("rwstg", [16, D], F32, scR)
        r_xsl = Res("xsl", fenceR)
        r_rwp = Res("rwp", fenceR)
        r_rT = Res("rT", fenceR)
        r_sco = Res("sco", fenceR)
        r_cho = Res("cho", fenceR)
        r_scot = Res("scot", fenceR)
        r_chot = Res("chot", fenceR)
        r_mx8 = Res("mx8", fenceR)
        r_msk = Res("msk", fenceR)
        r_cmb = Res("cmb", fenceR)
        r_den = Res("den", fenceR)
        r_ccm = Res("ccm", fenceR)
        r_rwstg = Res("rwstg", fenceR)
        r_combscr = Res("comb_scr")

        k.op("vector", lambda v: v.memset(rwp[:], 0.0), writes=[r_rwp])
        k.op("vector", lambda v: v.memset(sco[:], 0.0), writes=[r_sco])
        k.op("vector", lambda v: v.memset(cho[:], 0.0), writes=[r_cho])
        k.op("sync", lambda q: q.dma_start(out=rwstg[:], in_=rw_in),
             writes=[r_rwstg], dma=True)
        k.op("scalar", lambda s: s.copy(out=rwp[0:16, :], in_=rwstg[:]),
             reads=[r_rwstg], writes=[r_rwp])
        for kt in range(KT):
            k.op("tensor", lambda t, kt=kt: t.transpose(
                out=ps_bf[:, 0:128], in_=rwp[:, kt * 128:(kt + 1) * 128],
                identity=idb[:]), reads=[r_rwp, r_idb], writes=[r_psbf])
            k.op("scalar", lambda s, kt=kt: s.copy(
                out=rT[:, kt * 16:(kt + 1) * 16], in_=ps_bf[:, 0:16]),
                reads=[r_psbf], writes=[r_rT])
        for sl in range(4):
            k.op("sync", lambda q, sl=sl: q.dma_start(
                out=xsl[:], in_=xn2_scr_r[:, :, sl * 512:(sl + 1) * 512]),
                reads=r_xn2scr, writes=[r_xsl], dma=True)

            def mml(t, sl=sl):
                last = None
                for kt in range(KT):
                    last = t.matmul(
                        out=ps[4][0:16, :], lhsT=rT[:, kt * 16:(kt + 1) * 16],
                        rhs=xsl[:, kt, :], start=(kt == 0), stop=(kt == KT - 1))
                return last
            k.op("tensor", mml, reads=[r_rT, r_xsl], writes=[r_ps[4]])
            k.op("scalar", lambda s, sl=sl: s.activation(
                out=sco[0:16, sl * 512:(sl + 1) * 512], in_=ps[4][0:16, :],
                func=AF.Sigmoid), reads=[r_ps[4]], writes=[r_sco])
            k.op("vector", lambda v, sl=sl: v.tensor_scalar(
                out=cho[0:16, sl * 512:(sl + 1) * 512],
                in0=sco[0:16, sl * 512:(sl + 1) * 512],
                scalar1=rb_sb[0:16, 0:1], scalar2=None, op0=OP.add),
                reads=[r_sco, r_rb], writes=[r_cho])
        for tt in range(16):
            tsl = slice(tt * 128, (tt + 1) * 128)
            k.op("tensor", lambda t, tsl=tsl: t.transpose(
                out=ps[2][:, 0:128], in_=sco[:, tsl], identity=idf[:]),
                reads=[r_sco, r_idf], writes=[r_ps[2]])
            k.op("vector", lambda v: v.tensor_copy(
                out=scot[:], in_=ps[2][:, 0:16]),
                reads=[r_ps[2]], writes=[r_scot])
            k.op("tensor", lambda t, tsl=tsl: t.transpose(
                out=ps[3][:, 0:128], in_=cho[:, tsl], identity=idf[:]),
                reads=[r_cho, r_idf], writes=[r_ps[3]])
            k.op("vector", lambda v: v.tensor_copy(
                out=chot[:], in_=ps[3][:, 0:16]),
                reads=[r_ps[3]], writes=[r_chot])
            k.op("vector", lambda v: v.max(out=mx8[:], in_=chot[:]),
                 reads=[r_chot], writes=[r_mx8])
            k.op("vector", lambda v: v.tensor_scalar(
                out=msk[:], in0=chot[:], scalar1=mx8[:, 3:4], scalar2=None,
                op0=OP.is_ge), reads=[r_chot, r_mx8], writes=[r_msk])
            k.op("vector", lambda v: v.tensor_tensor(
                out=cmb[:], in0=scot[:], in1=msk[:], op=OP.mult),
                reads=[r_scot, r_msk], writes=[r_cmb])
            k.op("vector", lambda v: v.reduce_sum(
                den[:, 0:1], cmb[:], axis=AX.X),
                reads=[r_cmb], writes=[r_den])
            k.op("vector", lambda v: v.tensor_scalar(
                out=den[:, 0:1], in0=den[:, 0:1], scalar1=1e-20, scalar2=None,
                op0=OP.add), reads=[r_den], writes=[r_den])
            k.op("vector", lambda v: v.reciprocal(den[:, 0:1], den[:, 0:1]),
                 reads=[r_den], writes=[r_den])
            k.op("vector", lambda v: v.tensor_scalar(
                out=cmb[:], in0=cmb[:], scalar1=den[:, 0:1], scalar2=None,
                op0=OP.mult), reads=[r_cmb, r_den], writes=[r_cmb])
            k.op("tensor", lambda t: t.transpose(
                out=ps[5][0:16, 0:128], in_=cmb[:], identity=idf[:]),
                reads=[r_cmb, r_idf], writes=[r_ps[5]])
            k.op("vector", lambda v, tsl=tsl: v.tensor_copy(
                out=ccm[:, tsl], in_=ps[5][0:16, 0:128]),
                reads=[r_ps[5]], writes=[r_ccm])
        k.op("sync", lambda q: q.dma_start(out=comb_scr, in_=ccm[:]),
             reads=[r_ccm], writes=[r_combscr], dma=True)
        if "d_comb" in dbg:
            k.op("sync", lambda q: q.dma_start(out=dbg["d_comb"], in_=ccm[:]),
                 reads=[r_ccm], dma=True)

        scR.close()

        fenceC = k.fence()
        scC = st.enter_context(ExitStack())
        y_acc = sb("y_acc", [128, KT, CHUNK], F32, scC)
        slot = [sb(f"slot{i}", [128, KT * FF], BF16, scC) for i in range(2)]
        r_yacc = [Res(f"yacc{m}", fenceC) for m in range(KT)]
        r_slot = [[Res(f"slot{i}_{j}", fenceC) for j in range(2)]
                  for i in range(2)]

        def load_T(seq, e, which, t_w):
            si = seq % 2
            if which in ("g", "u"):
                scr = wg_scr if which == "g" else wu_scr
                for half in range(2):
                    queue = "sync" if half == 0 else "scalar"
                    k.op(queue, lambda q, e=e, half=half, si=si, scr=scr:
                         q.dma_start(
                             out=slot[si].ap()
                             .rearrange("p (kt f) -> p kt f", f=FF)
                             [:, half * 8:(half + 1) * 8, :],
                             in_=scr[e * D:(e + 1) * D, :]
                             .rearrange("(kt p) f -> p kt f", p=128)
                             [:, half * 8:(half + 1) * 8, :]),
                         extra_deps=t_w[half * 2:half * 2 + 2],
                         writes=[r_slot[si][half]], dma=True)
            else:
                for half in range(2):
                    queue = "sync" if half == 0 else "scalar"
                    k.op(queue, lambda q, e=e, half=half, si=si:
                         q.dma_start(
                             out=slot[si].ap()
                             .rearrange("p (fk h) -> p fk h", h=D)
                             [:, half * 4:(half + 1) * 4, :],
                             in_=wd_scr[e * FF:(e + 1) * FF, :]
                             .rearrange("(fk p) h -> p fk h", p=128)
                             [:, half * 4:(half + 1) * 4, :]),
                         extra_deps=t_w[half:half + 1],
                         writes=[r_slot[si][half]], dma=True)
            return si

        all_out_toks = []

        def emit_chunk(ch):
            c0_ = ch * CHUNK
            scE = st.enter_context(ExitStack())
            silu_g = sb(f"silu_g{ch}", [128, FT, CHUNK], BF16, scE)
            a_bf = sb(f"a_bf{ch}", [128, FT, CHUNK], BF16, scE)
            xs = sb(f"xs{ch}", [128, KT, CHUNK], BF16, scE)
            comb_bc = sb(f"comb_bc{ch}", [128, CHUNK], F32, scE)
            u512 = sb(f"u512_{ch}", [128, 512], F32, scE)
            fE = k.fence()
            r_silu = [Res(f"silu{ft}", fE) for ft in range(FT)]
            r_abf = [Res(f"a{ft}", fE) for ft in range(FT)]
            r_xs = Res("xs", fE)
            r_cbc = Res("comb_bc", fE)
            r_u512 = Res("u512", fE)
            k.op("sync", lambda q, c0_=c0_, xs=xs: q.dma_start(
                out=xs[:], in_=xn2_scr_r[:, :, c0_:c0_ + CHUNK]),
                reads=r_xn2scr, writes=[r_xs], dma=True)

            seq = 0
            for e in range(NE):
                si_g = load_T(seq, e, "g", t_wg[e]); seq += 1
                si_u = load_T(seq, e, "u", t_wu[e]); seq += 1
                if e < 16:
                    k.op("sync", lambda q, e=e, c0_=c0_: q.dma_start(
                        out=comb_bc[:],
                        in_=comb_scr[e:e + 1, c0_:c0_ + CHUNK]
                        .partition_broadcast(128)),
                        reads=[r_combscr], writes=[r_cbc], dma=True)
                for ft in range(FT):
                    for sl2 in range(CHUNK // 512):
                        pi = (ft * 2 + sl2) % 2

                        def mmg(t, ft=ft, sl2=sl2, pi=pi, si_g=si_g):
                            last = None
                            for kt in range(KT):
                                last = t.matmul(
                                    out=ps[pi][:],
                                    lhsT=slot[si_g][:, kt * FF + ft * 128:
                                                    kt * FF + (ft + 1) * 128],
                                    rhs=xs[:, kt, sl2 * 512:(sl2 + 1) * 512],
                                    start=(kt == 0), stop=(kt == KT - 1))
                            return last
                        k.op("tensor", mmg, reads=r_slot[si_g] + [r_xs],
                             writes=[r_ps[pi]])
                        k.op("scalar", lambda s, ft=ft, sl2=sl2, pi=pi:
                             s.activation(
                                 out=silu_g[:, ft, sl2 * 512:(sl2 + 1) * 512],
                                 in_=ps[pi][:], func=AF.Silu),
                             reads=[r_ps[pi]], writes=[r_silu[ft]])
                si_d = load_T(seq, e, "d", t_wd[e]); seq += 1
                for ft in range(FT):
                    for sl2 in range(CHUNK // 512):
                        pi = 2 + (ft * 2 + sl2) % 2

                        def mmu(t, ft=ft, sl2=sl2, pi=pi, si_u=si_u):
                            last = None
                            for kt in range(KT):
                                last = t.matmul(
                                    out=ps[pi][:],
                                    lhsT=slot[si_u][:, kt * FF + ft * 128:
                                                    kt * FF + (ft + 1) * 128],
                                    rhs=xs[:, kt, sl2 * 512:(sl2 + 1) * 512],
                                    start=(kt == 0), stop=(kt == KT - 1))
                            return last
                        k.op("tensor", mmu, reads=r_slot[si_u] + [r_xs],
                             writes=[r_ps[pi]])
                        ssl2 = slice(sl2 * 512, (sl2 + 1) * 512)
                        if e < 16:
                            k.op("vector", lambda v, pi=pi, ssl2=ssl2:
                                 v.tensor_tensor(
                                     out=u512[:], in0=ps[pi][:],
                                     in1=comb_bc[:, ssl2], op=OP.mult),
                                 reads=[r_ps[pi], r_cbc], writes=[r_u512])
                            k.op("vector", lambda v, ft=ft, ssl2=ssl2:
                                 v.tensor_tensor(
                                     out=a_bf[:, ft, ssl2], in0=u512[:],
                                     in1=silu_g[:, ft, ssl2], op=OP.mult),
                                 reads=[r_u512, r_silu[ft]],
                                 writes=[r_abf[ft]])
                        else:
                            k.op("vector", lambda v, ft=ft, pi=pi, ssl2=ssl2:
                                 v.tensor_tensor(
                                     out=a_bf[:, ft, ssl2], in0=ps[pi][:],
                                     in1=silu_g[:, ft, ssl2], op=OP.mult),
                                 reads=[r_ps[pi], r_silu[ft]],
                                 writes=[r_abf[ft]])
                dps = [ps[4], ps[5], ps_sq]
                rdps = [r_ps[4], r_ps[5], r_pssq]
                for ht in range(KT):
                    for sl2 in range(CHUNK // 512):
                        pi = (ht * 2 + sl2) % 3
                        pps, rpp = dps[pi], rdps[pi]

                        def mmd(t, ht=ht, sl2=sl2, pps=pps, si_d=si_d):
                            last = None
                            for fk in range(FT):
                                last = t.matmul(
                                    out=pps[:],
                                    lhsT=slot[si_d][:, fk * D + ht * 128:
                                                    fk * D + (ht + 1) * 128],
                                    rhs=a_bf[:, fk, sl2 * 512:(sl2 + 1) * 512],
                                    start=(fk == 0), stop=(fk == FT - 1))
                            return last
                        k.op("tensor", mmd, reads=r_slot[si_d][0:FT] + r_abf,
                             writes=[rpp])
                        ssl2 = slice(sl2 * 512, (sl2 + 1) * 512)
                        if e == 0:
                            k.op("vector", lambda v, ht=ht, pps=pps, ssl2=ssl2:
                                 v.tensor_copy(
                                     out=y_acc[:, ht, ssl2], in_=pps[:]),
                                 reads=[rpp], writes=[r_yacc[ht]])
                        else:
                            k.op("vector", lambda v, ht=ht, pps=pps, ssl2=ssl2:
                                 v.tensor_tensor(
                                     out=y_acc[:, ht, ssl2],
                                     in0=y_acc[:, ht, ssl2], in1=pps[:],
                                     op=OP.add),
                                 reads=[rpp, r_yacc[ht]],
                                 writes=[r_yacc[ht]])
            scE.close()
            if "d_y" in dbg:
                k.op("sync", lambda q, c0_=c0_: q.dma_start(
                    out=dbg["d_y"].rearrange("(kt p) t -> p kt t", p=128)
                    [:, :, c0_:c0_ + CHUNK], in_=y_acc[:]),
                    reads=r_yacc, dma=True)

            scD = st.enter_context(ExitStack())
            y4T = [sb(f"y4T{ch}_{i}", [128, D], F32, scD) for i in range(2)]
            x1r = [sb(f"x1r{ch}_{i}", [128, D], F32, scD) for i in range(2)]
            dumpD = [sb(f"dumpD{ch}_{i}", [128, D], F32, scD) for i in range(2)]
            fD = k.fence()
            r_y4T = [Res(f"y4T{i}", fD) for i in range(2)]
            r_x1r = [Res(f"x1r{i}", fD) for i in range(2)]
            r_dumpD = [Res(f"dumpD{i}", fD) for i in range(2)]
            for tb0 in range(0, CHUNK // 128, 2):
                tts = (ch * (CHUNK // 128) + tb0,
                       ch * (CHUNK // 128) + tb0 + 1)
                for i in range(2):
                    k.op("sync", lambda q, tt=tts[i], xb=x1r[i]: q.dma_start(
                        out=xb[:], in_=x1_scr[tt * 128:(tt + 1) * 128, :]),
                        reads=[r_x1scr[tts[i]]], writes=[r_x1r[i]], dma=True)
                for i in range(2):
                    tb = tb0 + i
                    for qd in range(4):
                        pb2 = 2 + qd % 2

                        def trd(t, qd=qd, tb=tb, pb2=pb2):
                            last = None
                            for j in range(4):
                                ht = qd * 4 + j
                                last = t.transpose(
                                    out=ps[pb2][:, j * 128:(j + 1) * 128],
                                    in_=y_acc[:, ht, tb * 128:(tb + 1) * 128],
                                    identity=idf[:])
                            return last
                        k.op("tensor", trd,
                             reads=[r_yacc[qd * 4 + j] for j in range(4)]
                             + [r_idf], writes=[r_ps[pb2]])
                        k.op("scalar", lambda s, qd=qd, pb2=pb2, yT=y4T[i]:
                             s.copy(out=yT[:, qd * 512:(qd + 1) * 512],
                                    in_=ps[pb2][:]),
                             reads=[r_ps[pb2]], writes=[r_y4T[i]])
                for i in range(2):
                    ln_pre(y4T[i], r_y4T[i], i)
                for i in range(2):
                    ln_post(i)
                for i in range(2):
                    ln_apply(y4T[i][:], r_y4T[i], dumpD[i][:], r_dumpD[i], i)
                for i in range(2):
                    k.op("vector", lambda v, xb=x1r[i], dp=dumpD[i]:
                         v.tensor_tensor(
                             out=dp[:], in0=dp[:], in1=xb[:], op=OP.add),
                         reads=[r_dumpD[i], r_x1r[i]], writes=[r_dumpD[i]])
                    all_out_toks.append(k.op(
                        "sync", lambda q, tt=tts[i], dp=dumpD[i]: q.dma_start(
                            out=out[tt * 128:(tt + 1) * 128, :], in_=dp[:]),
                        reads=[r_dumpD[i]], dma=True))
            scD.close()

        for ch in range(NCH):
            emit_chunk(ch)
        scC.close()

        blk = st.enter_context(nc.Block())
        k.finalize(blk, final_eng="sync", final_tokens=all_out_toks)
    return nc


def _panelize(w):
    return np.ascontiguousarray(
        np.asarray(w).reshape(KT, 128, KT, 128).transpose(2, 1, 0, 3)
        .reshape(D, D))


def _shared_in(inputs):
    gw = np.asarray(inputs["L0_ffn_expert_gate_w"])
    uw = np.asarray(inputs["L0_ffn_expert_up_w"])
    dw = np.asarray(inputs["L0_ffn_expert_down_w"])
    sg = np.asarray(inputs["L0_ffn_shared_gate_w"])
    su = np.asarray(inputs["L0_ffn_shared_up_w"])
    sd = np.asarray(inputs["L0_ffn_shared_down_w"])
    return {
        "wr": _panelize(inputs["L0_attn_Wr"]),
        "wf": _panelize(inputs["L0_attn_Wf"]),
        "wc": _panelize(inputs["L0_attn_Wc"]),
        "wo": _panelize(inputs["L0_attn_Wo"]),
        "lnsc": np.ascontiguousarray(
            np.asarray(inputs["L0_attn_ln_scale"]).reshape(KT, 128).T),
        "rw_in": np.asarray(inputs["L0_ffn_router_w"]),
        "rb_in": np.asarray(inputs["L0_ffn_router_bias"]),
        "wg_in": np.ascontiguousarray(np.concatenate(
            [gw.transpose(0, 2, 1).reshape(16 * D, FF), sg.T], axis=0)),
        "wu_in": np.ascontiguousarray(np.concatenate(
            [uw.transpose(0, 2, 1).reshape(16 * D, FF), su.T], axis=0)),
        "wd_in": np.ascontiguousarray(np.concatenate(
            [dw.transpose(0, 2, 1).reshape(16 * FF, D), sd.T], axis=0)),
        "id_bf": np.eye(128, dtype=ml_dtypes.bfloat16),
        "id_f32": np.eye(128, dtype=np.float32),
        "ones_f32": np.ones((128, 1), dtype=np.float32),
    }


def _in_map(inputs, b, shared=None):
    if shared is None:
        shared = _shared_in(inputs)
    x = np.asarray(inputs["L0_attn_x"])
    return {"x_in": np.ascontiguousarray(x[b]), **shared}


def kernel(**inputs):
    nc = build_nc()
    shared = _shared_in(inputs)
    in_maps = [_in_map(inputs, b, shared) for b in range(NCORES)]
    res = run_bass_kernel_spmd(nc, in_maps, core_ids=list(range(NCORES)))
    return np.stack([r["out"] for r in res.results], axis=0)
